# version 16
# baseline (speedup 1.0000x reference)
import numpy as np
import ml_dtypes
import concourse.bass as bass
import concourse.mybir as mybir
from concourse.bass_utils import run_bass_kernel_spmd

F32 = mybir.dt.float32
BF16 = mybir.dt.bfloat16
AF = mybir.ActivationFunctionType
ALU = mybir.AluOpType
AXX = mybir.AxisListType.X

D = 2048
DC = 16
TQ = 512
NST = 16
H = 1024
HC = 8
NE = 9
EPS = 1e-5

_CACHE = {}


class Sem:
    def __init__(self, stack, nc, name):
        self.h = stack.enter_context(nc.semaphore(name))
        self.n = 0

    def inc(self, k=1):
        self.n += k
        return self.n


class Prog:
    ENGS = ("sync", "gpsimd", "tensor", "vector", "scalar")

    def __init__(self, nc, stack):
        self.nc = nc
        self.stack = stack
        self.q = {e: [] for e in self.ENGS}
        self._sems = {}

    def sem(self, name):
        if name not in self._sems:
            self._sems[name] = Sem(self.stack, self.nc, name)
        return self._sems[name]

    def op(self, eng, fn, waits=(), inc=None, inc_k=1):
        w = [(s.h, v) for (s, v) in waits if v > 0]
        n = None
        if inc is not None:
            n = inc.inc(inc_k)
        self.q[eng].append((w, fn, inc.h if inc else None, inc_k))
        return n

    def wait(self, eng, sem, val):
        self.q[eng].append(([(sem.h, val)], None, None, 0))

    def barrier(self, name):
        s = self.sem(name)
        for e in self.ENGS:
            self.op(e, lambda h: h.nop(), inc=s)
        tgt = s.n
        for e in self.ENGS:
            self.wait(e, s, tgt)

    def replay(self):
        counts = {e: len(self.q[e]) for e in self.ENGS}
        with self.nc.Block() as block:
            def runner(eng_name):
                def run(h):
                    for (w, fn, inc_h, inc_k) in self.q[eng_name]:
                        for (sh, v) in w:
                            h.wait_ge(sh, v)
                        if fn is not None:
                            ins = fn(h)
                            if inc_h is not None:
                                ins.then_inc(inc_h, inc_k)
                        elif inc_h is not None:
                            h.nop().then_inc(inc_h, inc_k)
                return run
            block.sync(runner("sync"))
            block.gpsimd(runner("gpsimd"))
            block.tensor(runner("tensor"))
            block.vector(runner("vector"))
            block.scalar(runner("scalar"))
        return counts


def build(nq=4, debug_spills=False):
    from contextlib import ExitStack
    nc = bass.Bass("TRN2", target_bir_lowering=False, debug=False,
                   detect_race_conditions=False)
    stack = ExitStack()
    Tn = nq * TQ
    NT128 = Tn // 128

    din = {}
    def dram_in(name, shape, dtype=F32):
        din[name] = nc.dram_tensor(name, list(shape), dtype,
                                   kind="ExternalInput").ap()
        return din[name]

    xT = dram_in("xT", (D, Tn))
    wmod = dram_in("wmod", (D, 6 * D))
    c16 = dram_in("c16", (128, DC))
    gateT = dram_in("gateT", (D, 8))
    xprojT = dram_in("xprojT", (D, 160))
    dtprojT = dram_in("dtprojT", (128, D))
    dtb = dram_in("dtb", (128, DC))
    convw = dram_in("convw", (128, DC, 4))
    convb = dram_in("convb", (128, DC))
    alog = dram_in("alog", (128, DC, NST))
    dskip = dram_in("dskip", (128, DC))
    W1 = dram_in("W1", (NE, D, H))
    W3 = dram_in("W3", (NE, D, H))
    W2s = dram_in("W2s", (DC, NE, HC, 128, 128))
    eye = dram_in("eye", (128, 128))
    onesv = dram_in("onesv", (128, 1))

    outT = nc.dram_tensor("outT", [D, Tn], F32, kind="ExternalOutput").ap()
    stat_hbm = nc.dram_tensor("stat_hbm", [2, TQ], F32, kind="Internal").ap()
    bc_hbm = nc.dram_tensor("bc_hbm", [32, TQ], F32, kind="Internal").ap()
    mT_hbm = nc.dram_tensor("mT_hbm", [NE, 8 * TQ], BF16,
                            kind=("ExternalOutput" if debug_spills else "Internal")).ap()
    if debug_spills:
        dbg_m = nc.dram_tensor("dbg_m", [128, Tn // 128, 8], F32,
                               kind="ExternalOutput").ap()
        dbg_sc8 = nc.dram_tensor("dbg_sc8", [8, TQ], F32,
                                 kind="ExternalOutput").ap()
    dbg_kind = "ExternalOutput" if debug_spills else "Internal"
    x2_sp = nc.dram_tensor("x2_sp", [128, DC, Tn], F32, kind=dbg_kind).ap()
    h2_sp = nc.dram_tensor("h2_sp", [128, DC, Tn], BF16, kind=dbg_kind).ap()
    g_sp = nc.dram_tensor("g_sp", [NE, HC, 128, Tn], BF16, kind=dbg_kind).ap()
    W2b = nc.dram_tensor("W2b", [DC, NE, HC, 128, 128], BF16, kind="Internal").ap()

    p = Prog(nc, stack)
    S = p.sem

    sb = lambda *a: stack.enter_context(nc.sbuf_tensor(*a))
    ones_sb = sb("ones_sb", [128, 1], F32)
    eye_sb = sb("eye_sb", [128, 128], F32)
    c_sb = sb("c_sb", [128, DC], F32)
    silc_sb = sb("silc_sb", [128, DC], F32)
    gateT_sb = sb("gateT_sb", [128, DC, 8], F32)
    xprojT_sb = sb("xprojT_sb", [128, DC, 160], BF16)
    dtprojT_sb = sb("dtprojT_sb", [128, DC, 128], BF16)
    dtb_sb = sb("dtb_sb", [128, DC], F32)
    convw_sb = sb("convw_sb", [128, DC, 4], F32)
    convb_sb = sb("convb_sb", [128, DC], F32)
    alog_sb = sb("alog_sb", [128, DC, NST], F32)
    A_sb = alog_sb
    dskip_sb = sb("dskip_sb", [128, DC], F32)
    modv = sb("modv", [128, 6, DC], F32)
    onca = sb("onca", [128, DC], F32)
    oncb = sb("oncb", [128, DC], F32)
    gac = sb("gac", [128, DC], F32)
    gbc = sb("gbc", [128, DC], F32)
    sav = sb("sav", [128, DC], F32)
    sbv = sb("sbv", [128, DC], F32)
    m_sb = sb("m_sb", [128, NT128, 8], F32)
    mT_sb = sb("mT_sb", [NE, Tn], BF16)
    halo = sb("halo", [128, DC, 3], F32)
    carry = sb("carry", [128, DC * NST], F32)
    kconst = sb("kconst", [128, 4], F32)
    ones8 = sb("ones8", [128, 8], F32)
    onesb = sb("onesb", [128, 1], BF16)
    ps = stack.enter_context(nc.psum_tensor("ps", [128, 8, 512], F32))

    def bank(b, m=128, n=512):
        return ps[0:m, b, 0:n]

    s_c = S("s_c"); s_cg = S("s_cg"); s_sil = S("s_sil")
    s_wm = S("s_wm"); s_wmf = S("s_wmf")
    s_gemv = S("s_gemv"); s_gemvf = S("s_gemvf")
    s_mrow = S("s_mrow"); s_modv = S("s_modv"); s_modr = S("s_modr")

    def ld(eng, dst, src, sem):
        return p.op(eng, lambda h, d=dst, s=src: h.dma_start(out=d, in_=s),
                    inc=sem, inc_k=16)

    ld("sync", ones_sb[:], onesv, s_c)
    ld("sync", eye_sb[:], eye, s_c)
    ld("sync", c_sb[:], c16, s_c)
    ld("sync", gateT_sb[:], gateT.rearrange("(c p) e -> p c e", p=128), s_c)
    ld("sync", dtb_sb[:], dtb, s_c)
    ld("sync", convw_sb[:], convw, s_c)
    ld("sync", convb_sb[:], convb, s_c)
    ld("sync", alog_sb[:], alog, s_c)
    ld("sync", dskip_sb[:], dskip, s_c)
    n_c = s_c.n
    ld("gpsimd", xprojT_sb[:], xprojT.rearrange("(c p) e -> p c e", p=128), s_cg)
    ld("gpsimd", dtprojT_sb[:], dtprojT.rearrange("r (c j) -> r c j", c=DC), s_cg)
    n_cg = s_cg.n

    p.op("scalar", lambda h: h.activation(silc_sb[:], c_sb[:], AF.Silu),
         waits=[(s_c, n_c)], inc=s_sil)
    p.op("scalar", lambda h: h.activation(A_sb[:], alog_sb[:], AF.Exp), inc=s_sil)
    p.op("vector", lambda h: h.tensor_scalar_mul(A_sb[:], A_sb[:], -1.0),
         waits=[(s_sil, 2), (s_c, n_c)], inc=s_modr)
    p.op("vector", lambda h: h.memset(halo[:], 0.0), inc=s_modr)
    p.op("vector", lambda h: h.memset(carry[:], 0.0), inc=s_modr)
    p.op("vector", lambda h: h.memset(kconst[:, 0:1], EPS), inc=s_modr)
    p.op("vector", lambda h: h.memset(kconst[:, 1:2], -0.5), inc=s_modr)
    p.op("vector", lambda h: h.memset(kconst[:, 2:3], -1.0), inc=s_modr)
    p.op("vector", lambda h: h.memset(kconst[:, 3:4], 1.0), inc=s_modr)
    p.op("vector", lambda h: h.memset(ones8[:], 1.0), inc=s_modr)
    p.op("vector", lambda h: h.memset(onesb[:], 1.0), inc=s_modr)

    with nc.sbuf_tensor("wmbuf", [128, 2, DC, 512], F32) as wmbuf, \
         nc.sbuf_tensor("modrow", [1, 6 * D], F32) as modrow:
        njt = 6 * D // 512
        for jt in range(njt):
            buf = jt % 2
            w = [(s_wmf, 16 * (jt - 1))] if jt >= 2 else []
            ld_n = p.op("sync",
                        lambda h, b=buf, j=jt: h.dma_start(
                            out=wmbuf[:, b, :, :],
                            in_=wmod[:, j * 512:(j + 1) * 512]
                            .rearrange("(c p) j -> p c j", p=128)),
                        waits=w, inc=s_wm, inc_k=16)
            bk = 6 + jt % 2
            for dc in range(DC):
                wt = []
                if dc == 0:
                    wt = [(s_wm, ld_n), (s_sil, 1)]
                    if jt >= 2:
                        wt.append((s_gemvf, jt - 1))
                p.op("tensor",
                     lambda h, b=buf, d=dc, k=bk: h.matmul(
                         bank(k, 1), silc_sb[:, d:d + 1], wmbuf[:, b, d, :],
                         start=(d == 0), stop=(d == DC - 1)),
                     waits=wt, inc=(s_gemv if dc == DC - 1 else None))
            p.op("vector",
                 lambda h, j=jt, k=bk: h.tensor_copy(
                     modrow[:, j * 512:(j + 1) * 512], bank(k, 1)),
                 waits=[(s_gemv, jt + 1)], inc=s_gemvf)
            p.op("vector", lambda h: h.nop(), inc=s_wmf, inc_k=16)
        s_mtr = S("s_mtr"); s_mtrv = S("s_mtrv")
        for v in range(6):
            bk = 5 + 2 * (v % 2)
            for cc in range(DC):
                gj = v * DC + cc
                wt = []
                if gj == 0:
                    wt.append((s_gemvf, njt))
                    wt.append((s_c, n_c))
                if v >= 2 and cc == 0:
                    wt.append((s_mtrv, v - 1))
                p.op("tensor", lambda h, k=bk, j=gj, c=cc: h.transpose(
                         ps[0:128, k, c:c + 1],
                         modrow[:, j * 128:(j + 1) * 128],
                         eye_sb[0:1, 0:1]),
                     waits=wt, inc=(s_mtr if cc == DC - 1 else None))
            p.op("vector", lambda h, k=bk, vv=v: h.tensor_copy(
                     modv[:, vv, :], ps[0:128, k, 0:DC]),
                 waits=[(s_mtr, s_mtr.n)], inc=s_mtrv)
        n_ev = s_mtrv.n

    def drv(fn):
        p.op("vector", fn, waits=[(s_mtrv, n_ev)], inc=s_modr)
    drv(lambda h: h.tensor_copy(sav[:], modv[:, 0, :]))
    drv(lambda h: h.tensor_scalar_add(onca[:], modv[:, 1, :], 1.0))
    drv(lambda h: h.tensor_copy(sbv[:], modv[:, 3, :]))
    drv(lambda h: h.tensor_scalar_add(oncb[:], modv[:, 4, :], 1.0))
    with nc.sbuf_tensor("clt", [128, 3, DC], F32) as clt:
        for (src_i, dst) in ((2, gac), (5, gbc)):
            drv(lambda h, i=src_i: h.tensor_scalar_max(
                    clt[:, 0, :], modv[:, i, :], 0.25))
            drv(lambda h, i=src_i: h.tensor_scalar_min(
                    clt[:, 1, :], modv[:, i, :], -0.25))
            drv(lambda h, i=src_i: h.tensor_scalar(
                    clt[:, 2, :], modv[:, i, :], 0.0, None, op0=ALU.is_ge))
            drv(lambda h: h.tensor_sub(clt[:, 0, :], clt[:, 0, :], clt[:, 1, :]))
            drv(lambda h: h.tensor_mul(clt[:, 2, :], clt[:, 2, :], clt[:, 0, :]))
            drv(lambda h, d=dst: h.tensor_add(d[:], clt[:, 1, :], clt[:, 2, :]))
    n_modr = s_modr.n

    p.barrier("b_ph0")

    s_w2c = S("s_w2c")
    for dc in range(DC):
        p.op("gpsimd", lambda h, d=dc: h.dma_start(out=W2b[d], in_=W2s[d]),
             inc=s_w2c, inc_k=16)
    n_w2c = s_w2c.n

    from contextlib import ExitStack as ES
    ph1 = ES()
    sb1 = lambda *a: ph1.enter_context(nc.sbuf_tensor(*a))
    statb = sb1("statb", [1, 3, TQ], F32)
    x_q = sb1("x_q", [128, DC, TQ], F32)
    u_bf = sb1("u_bf", [128, DC, TQ], BF16)
    delta_q = sb1("delta_q", [128, DC, TQ], F32)
    du_q = sb1("du_q", [128, DC, TQ], BF16)
    y_q = sb1("y_q", [128, DC, TQ], F32)
    bcrep = sb1("bcrep", [128, 4, 2, TQ], BF16)
    dA_t = sb1("dA_t", [128, 2, TQ], F32)
    dBu_t = sb1("dBu_t", [128, TQ], BF16)
    h_t = sb1("h_t", [128, TQ], BF16)
    tmp_t = sb1("tmp_t", [128, TQ], BF16)
    mr1 = sb1("mr1", [128, 2, TQ], F32)
    mr2 = mr1
    h1x = sb1("h1x", [128, 2, TQ + 3], BF16)
    h1f = sb1("h1f", [128, 2, TQ], F32)
    sq_t = sb1("sq_t", [128, 2, TQ], BF16)
    u_t = sb1("u_t", [128, 2, TQ], F32)
    sp_t = u_t
    dtlow = sb1("dtlow", [128, TQ], BF16)
    bc_sb = sb1("bc_sb", [32, TQ], F32)
    x2g = sb1("x2g", [128, 2, 2, TQ], F32)
    x2in = x2g
    h2g = sb1("h2g", [128, 2, 2, TQ], BF16)
    h2f_t = sb1("h2f_t", [128, 2, TQ], F32)
    sc_sb = sb1("sc_sb", [128, NT128, 8], F32)
    sc8 = sb1("sc8", [8, TQ], F32)
    mk = sb1("mk", [128, NT128, 6, 8], F32)

    s_x = S("s_x"); s_xf = S("s_xf")
    s_sq1 = S("s_sq1"); s_smm1 = S("s_smm1")
    s_st1 = S("s_st1"); s_st1v = S("s_st1v"); s_st1f = S("s_st1f")
    s_mr1 = S("s_mr1"); s_mr1f = S("s_mr1f")
    s_vap = S("s_vap"); s_h1 = S("s_h1"); s_cv = S("s_cv"); s_ubf = S("s_ubf")
    s_xp = S("s_xp"); s_dtl = S("s_dtl"); s_bcv = S("s_bcv")
    s_dmm = S("s_dmm"); s_spf = S("s_spf")
    s_sp4 = S("s_sp4"); s_dlt = S("s_dlt")
    s_bcr = S("s_bcr"); s_bcrf = S("s_bcrf")
    s_dA = S("s_dA"); s_dAf = S("s_dAf")
    s_x2v = S("s_x2v"); s_x2w = S("s_x2w"); s_x2t = S("s_x2t")
    s_sq2 = S("s_sq2"); s_smm2 = S("s_smm2")
    s_st2 = S("s_st2"); s_st2v = S("s_st2v"); s_st2f = S("s_st2f")
    s_mr2 = S("s_mr2"); s_mr2f = S("s_mr2f")
    s_x2i = S("s_x2i"); s_x2if = S("s_x2if")
    s_vap2 = S("s_vap2"); s_h2f = S("s_h2f")
    s_gmm = S("s_gmm"); s_gsig = S("s_gsig"); s_gbf = S("s_gbf")
    s_trv = S("s_trv"); s_trin = S("s_trin")
    s_ka = S("s_ka"); s_kb = S("s_kb")
    s_h2v = S("s_h2v"); s_h2w = S("s_h2w")
    s_msk = S("s_msk"); s_trp = S("s_trp"); s_trpf = S("s_trpf")
    s_stw = S("s_stw"); s_bcw = S("s_bcw"); s_mtw = S("s_mtw")

    x2w_cnt = []
    h2w_cnt = []

    def fin_stats(stat, bsum, bsq, s_in, n_in, s_out, s_free, extra):
        p.op("vector", lambda h: h.tensor_scalar_mul(
                 stat[:, 0, :], bank(bsum, 1), 1.0 / D),
             waits=[(s_in, n_in)] + extra)
        p.op("vector", lambda h: h.tensor_scalar_mul(
                 stat[:, 1, :], bank(bsq, 1), 1.0 / D), inc=s_free)
        p.op("vector", lambda h: h.tensor_mul(
                 stat[:, 2, :], stat[:, 0, :], stat[:, 0, :]))
        p.op("vector", lambda h: h.tensor_sub(
                 stat[:, 1, :], stat[:, 1, :], stat[:, 2, :]), inc=s_out)
        p.op("scalar", lambda h: h.activation(
                 stat[:, 1, :], stat[:, 1, :], AF.Ln, bias=kconst[0:1, 0:1]),
             waits=[(s_out, s_out.n)])
        p.op("scalar", lambda h: h.activation(
                 stat[:, 1, :], stat[:, 1, :], AF.Exp,
                 scale=kconst[0:1, 1:2]), inc=s_out)

    for q in range(nq):
        wx = [(s_xf, q)] if q >= 1 else [(s_modr, n_modr)]
        nx = p.op("sync", lambda h, qq=q: h.dma_start(
                      out=x_q[:], in_=xT[:, qq * TQ:(qq + 1) * TQ]
                      .rearrange("(c p) t -> p c t", p=128)),
                  waits=wx, inc=s_x, inc_k=16)

        for dc in range(DC):
            wt = [(s_x, nx)] if dc == 0 else []
            if dc >= 2:
                wt.append((s_smm1, s_smm1.n - 1))
            elif q >= 1:
                wt.append((s_st2, q))
            p.op("scalar", lambda h, d=dc: h.activation(
                     sq_t[:, d % 2, :], x_q[:, d, :], AF.Square),
                 waits=wt, inc=s_sq1)
            wt = [(s_x, nx), (s_c, n_c)] if dc == 0 else []
            if dc == 0 and q >= 1:
                wt.append((s_st1f, q))
            p.op("tensor", lambda h, d=dc: h.matmul(
                     bank(0, 1), ones_sb[:], x_q[:, d, :],
                     start=(d == 0), stop=(d == DC - 1)), waits=wt)
            p.op("tensor", lambda h, d=dc: h.matmul(
                     bank(1, 1), onesb[:], sq_t[:, d % 2, :],
                     start=(d == 0), stop=(d == DC - 1)),
                 waits=[(s_sq1, s_sq1.n)],
                 inc=(s_st1 if dc == DC - 1 else s_smm1))
            if dc == DC - 1:
                s_smm1.inc(0)
        extra = [(s_stw, 32 * q)] if q >= 1 else []
        fin_stats(statb, 0, 1, s_st1, s_st1.n, s_st1v, s_st1f, extra)
        wsw = [(s_st1v, s_st1v.n), (s_mr2, 16 * q)]
        nsw1 = p.op("sync", lambda h: h.dma_start(
                        out=stat_hbm, in_=statb[0:1, 0:2, :]),
                    waits=wsw, inc=s_stw, inc_k=16)
        wmr = [(s_stw, nsw1)]
        if q >= 1:
            wmr.append((s_mr1f, q))
            wmr.append((s_mr2f, q))
        nmr1 = p.op("gpsimd", lambda h: h.dma_start(
                        out=mr1[:],
                        in_=stat_hbm.partition_broadcast(128)),
                    waits=wmr, inc=s_mr1, inc_k=16)

        for dc in range(DC):
            par = dc % 2
            wt = [(s_mr1, nmr1)] if dc == 0 else []
            if dc >= 2:
                wt.append((s_h1, s_h1.n - 1))
            p.op("vector", lambda h, d=dc, b=par: h.tensor_sub(
                     h1f[:, b, :], x_q[:, d, :], mr1[:, 0, :]), waits=wt)
            p.op("vector", lambda h, b=par: h.tensor_mul(
                     h1f[:, b, :], h1f[:, b, :], mr1[:, 1, :]),
                 inc=s_vap)
            if dc == DC - 1:
                p.op("vector", lambda h: h.nop(), inc=s_mr1f)
            p.op("scalar", lambda h, d=dc, b=par: h.activation(
                     h1x[:, b, 3:3 + TQ], h1f[:, b, :], AF.Identity,
                     bias=sav[:, d:d + 1], scale=onca[:, d:d + 1]),
                 waits=[(s_vap, s_vap.n)], inc=s_h1)
            p.op("vector", lambda h, d=dc, b=par: h.tensor_copy(
                     h1x[:, b, 0:3], halo[:, d, :]),
                 waits=[(s_h1, s_h1.n)])
            p.op("vector", lambda h, d=dc, b=par: h.tensor_copy(
                     halo[:, d, :], h1x[:, b, TQ:TQ + 3]))
            wt = [(s_ubf, s_ubf.n - 1)] if dc >= 2 else []
            p.op("vector", lambda h, d=dc, b=par: h.tensor_scalar(
                     u_t[:, d % 2, :], h1x[:, b, 3:3 + TQ],
                     convw_sb[:, d, 3:4], convb_sb[:, d:d + 1],
                     op0=ALU.mult, op1=ALU.add), waits=wt)
            for j in (2, 1, 0):
                p.op("vector", lambda h, d=dc, b=par, jj=j: h.scalar_tensor_tensor(
                         u_t[:, d % 2, :], h1x[:, b, jj:jj + TQ],
                         convw_sb[:, d, jj:jj + 1], u_t[:, d % 2, :],
                         op0=ALU.mult, op1=ALU.add),
                     inc=(s_cv if j == 0 else None))
            p.op("scalar", lambda h, d=dc: h.copy(u_bf[:, d, :], u_t[:, d % 2, :]),
                 waits=[(s_cv, s_cv.n)], inc=s_ubf)
            wt = [(s_ubf, s_ubf.n)]
            if dc == 0:
                if q >= 1:
                    wt.append((s_spf, 16 * q))
                    wt.append((s_trv, 4 * q))
                else:
                    wt.append((s_cg, n_cg))
            p.op("tensor", lambda h, d=dc: h.matmul(
                     bank(2), xprojT_sb[:, d, 0:128], u_bf[:, d, :],
                     start=(d == 0), stop=(d == DC - 1)), waits=wt)
            wt2 = []
            if dc == 0 and q >= 1:
                wt2.append((s_gbf, q))
            p.op("tensor", lambda h, d=dc: h.matmul(
                     bank(4, 32), xprojT_sb[:, d, 128:160], u_bf[:, d, :],
                     start=(d == 0), stop=(d == DC - 1)),
                 waits=wt2, inc=(s_xp if dc == DC - 1 else None))

        p.op("scalar", lambda h: h.copy(dtlow[:], bank(2)),
             waits=[(s_xp, s_xp.n), (s_dmm, s_dmm.n)], inc=s_dtl)
        p.op("vector", lambda h: h.tensor_copy(bc_sb[:], bank(4, 32)),
             waits=[(s_xp, s_xp.n), (s_bcw, 16 * q)], inc=s_bcv)
        nbw = p.op("sync", lambda h: h.dma_start(out=bc_hbm, in_=bc_sb[:]),
                   waits=[(s_bcv, s_bcv.n), (s_bcr, s_bcr.n)],
                   inc=s_bcw, inc_k=16)
        for dc in range(DC):
            bk = 2 + dc % 2
            wt = [(s_dtl, s_dtl.n)] if dc < 2 else [(s_spf, s_spf.n - 1)]
            if dc < 2 and q >= 1:
                wt.append((s_gbf, q))
                wt.append((s_trv, 4 * q))
            p.op("tensor", lambda h, d=dc, k=bk: h.matmul(
                     bank(k), dtprojT_sb[:, d, :], dtlow[:],
                     start=True, stop=True), waits=wt, inc=s_dmm)
            wsp = [(s_dmm, s_dmm.n)]
            if dc >= 1 or q >= 1:
                wsp.append((s_dlt, s_dlt.n))
            p.op("scalar", lambda h, d=dc, k=bk: h.activation(
                     sp_t[:, 0, :], bank(k), AF.Abs, bias=dtb_sb[:, d:d + 1]),
                 waits=wsp)
            p.op("scalar", lambda h, d=dc, k=bk: h.activation(
                     sp_t[:, 1, :], bank(k), AF.Relu, bias=dtb_sb[:, d:d + 1]),
                 inc=s_spf)
            p.op("scalar", lambda h: h.activation(
                     sp_t[:, 0, :], sp_t[:, 0, :], AF.Exp,
                     scale=kconst[:, 2:3]))
            p.op("scalar", lambda h: h.activation(
                     sp_t[:, 0, :], sp_t[:, 0, :], AF.Ln,
                     bias=kconst[:, 3:4]), inc=s_sp4)
            p.op("vector", lambda h, d=dc: h.tensor_add(
                     delta_q[:, d, :], sp_t[:, 1, :], sp_t[:, 0, :]),
                 waits=[(s_sp4, s_sp4.n)], inc=s_dlt)

        for dc in range(DC):
            wt = [(s_spf, s_spf.n - DC + dc + 1)]
            p.op("vector", lambda h, d=dc: h.tensor_mul(
                     du_q[:, d, :], delta_q[:, d, :], u_bf[:, d, :]), waits=wt)
            p.op("vector", lambda h, d=dc: h.tensor_scalar_mul(
                     y_q[:, d, :], u_bf[:, d, :], dskip_sb[:, d:d + 1]))
        it0 = DC * NST * q
        for n in range(NST):
            buf = n % 4
            gidx = q * NST + n
            wb = [(s_bcw, nbw)]
            if gidx >= 4:
                wb.append((s_bcrf, gidx - 3))
            p.op("gpsimd", lambda h, b=buf, nn=n: h.dma_start(
                     out=bcrep[:, b, 0, :],
                     in_=bc_hbm[nn:nn + 1, :].partition_broadcast(128)),
                 waits=wb, inc=s_bcr, inc_k=16)
            nb2 = p.op("gpsimd", lambda h, b=buf, nn=n: h.dma_start(
                           out=bcrep[:, b, 1, :],
                           in_=bc_hbm[16 + nn:17 + nn, :].partition_broadcast(128)),
                       inc=s_bcr, inc_k=16)
            for dc in range(DC):
                i = it0 + n * DC + dc
                dab = i % 2
                wda = [(s_dAf, i - 1)]
                if n == 0 and dc == 0:
                    wda.append((s_dlt, s_dlt.n))
                p.op("scalar", lambda h, d=dc, nn=n, b=dab: h.activation(
                         dA_t[:, b, :], delta_q[:, d, :], AF.Exp,
                         scale=A_sb[:, d, nn:nn + 1]),
                     waits=wda, inc=s_dA)
                wv = [(s_bcr, nb2)] if dc == 0 else []
                p.op("vector", lambda h, d=dc, b=buf: h.tensor_mul(
                         dBu_t[:], du_q[:, d, :], bcrep[:, b, 0, :]), waits=wv)
                ci = dc * NST + n
                p.op("vector", lambda h, b=dab, c=ci: h.tensor_tensor_scan(
                         h_t[:], dA_t[:, b, :], dBu_t[:], carry[:, c:c + 1],
                         op0=ALU.mult, op1=ALU.add),
                     waits=[(s_dA, s_dA.n)], inc=s_dAf)
                p.op("vector", lambda h, c=ci: h.tensor_copy(
                         carry[:, c:c + 1], h_t[:, TQ - 1:TQ]))
                p.op("vector", lambda h, b=buf: h.tensor_mul(
                         tmp_t[:], h_t[:], bcrep[:, b, 1, :]))
                p.op("vector", lambda h, d=dc: h.tensor_add(
                         y_q[:, d, :], y_q[:, d, :], tmp_t[:]),
                     inc=(s_bcrf if dc == DC - 1 else None))

        for dc in range(DC):
            gi = dc // 2
            gg = q * 8 + gi
            gbuf = gi % 2
            j = q * DC + dc
            wt = []
            if dc % 2 == 0 and gg >= 2:
                wt.append((s_x2w, x2w_cnt[gg - 2]))
            if j >= 4:
                wt.append((s_x2t, j - 3))
                wt.append((s_sq2, j - 3))
            if dc < 2:
                wt.append((s_st1, q + 1))
            p.op("vector", lambda h, d=dc, b=gbuf: h.scalar_tensor_tensor(
                     x2g[:, b, d % 2, :], y_q[:, d, :], gac[:, d:d + 1],
                     x_q[:, d, :], op0=ALU.mult, op1=ALU.add),
                 waits=wt, inc=s_x2v)
            if dc == DC - 1:
                p.op("vector", lambda h: h.nop(), inc=s_xf)
            wt = [(s_x2v, s_x2v.n)]
            if dc >= 2:
                wt.append((s_smm2, s_smm2.n - 1))
            else:
                wt.append((s_st1, q + 1))
            p.op("scalar", lambda h, d=dc, b=gbuf: h.activation(
                     sq_t[:, d % 2, :], x2g[:, b, d % 2, :], AF.Square),
                 waits=wt, inc=s_sq2)
            wt = [(s_x2v, s_x2v.n)]
            if dc == 0 and q >= 1:
                wt.append((s_st2f, q))
                wt.append((s_gbf, q))
            p.op("tensor", lambda h, d=dc, b=gbuf: h.matmul(
                     bank(5, 1), ones_sb[:], x2g[:, b, d % 2, :],
                     start=(d == 0), stop=(d == DC - 1)),
                 waits=wt, inc=(s_x2t if dc < DC - 1 else None))
            p.op("tensor", lambda h, d=dc: h.matmul(
                     bank(6, 1), onesb[:], sq_t[:, d % 2, :],
                     start=(d == 0), stop=(d == DC - 1)),
                 waits=[(s_sq2, s_sq2.n)],
                 inc=(s_st2 if dc == DC - 1 else s_smm2))
            if dc == DC - 1:
                p.op("tensor", lambda h: h.nop(), inc=s_x2t)
            if dc % 2 == 1:
                nw = p.op("sync", lambda h, qq=q, g=gi, b=gbuf: h.dma_start(
                              out=x2_sp[:, g * 2:(g + 1) * 2,
                                        qq * TQ:(qq + 1) * TQ],
                              in_=x2g[:, b, :, :]),
                          waits=[(s_x2v, s_x2v.n)], inc=s_x2w, inc_k=16)
                x2w_cnt.append(nw)
        extra = [(s_stw, 32 * q + 16)]
        fin_stats(statb, 5, 6, s_st2, s_st2.n, s_st2v, s_st2f, extra)
        wsw = [(s_st2v, s_st2v.n), (s_mr1, 16 * (q + 1))]
        nsw2 = p.op("sync", lambda h: h.dma_start(
                        out=stat_hbm, in_=statb[0:1, 0:2, :]),
                    waits=wsw, inc=s_stw, inc_k=16)
        wmr = [(s_stw, nsw2), (s_mr1f, q + 1)]
        if q >= 1:
            wmr.append((s_mr2f, q))
        nmr2 = p.op("gpsimd", lambda h: h.dma_start(
                        out=mr2[:],
                        in_=stat_hbm.partition_broadcast(128)),
                    waits=wmr, inc=s_mr2, inc_k=16)

        for dc in range(DC):
            gi = dc // 2
            gg = q * 8 + gi
            gbuf = gi % 2
            if dc % 2 == 0:
                wt = [(s_x2w, s_x2w.n),
                      (s_x2t, s_x2t.n)]
                if gg >= 2:
                    wt.append((s_x2if, gg - 1))
                p.op("sync", lambda h, qq=q, g=gi, b=gbuf: h.dma_start(
                         out=x2in[:, b, :, :],
                         in_=x2_sp[:, g * 2:(g + 1) * 2,
                                   qq * TQ:(qq + 1) * TQ]),
                     waits=wt, inc=s_x2i, inc_k=16)
            wt = [(s_x2i, s_x2i.n), (s_mr2, nmr2)] if dc % 2 == 0 else []
            if dc >= 2:
                wt.append((s_h2f, s_h2f.n - 1))
            p.op("vector", lambda h, d=dc, b=gbuf: h.tensor_sub(
                     h1f[:, d % 2, :], x2in[:, b, d % 2, :], mr2[:, 0, :]),
                 waits=wt, inc=(s_x2if if dc % 2 == 1 else None))
            p.op("vector", lambda h, d=dc: h.tensor_mul(
                     h1f[:, d % 2, :], h1f[:, d % 2, :], mr2[:, 1, :]),
                 inc=s_vap2)
            if dc == DC - 1:
                p.op("vector", lambda h: h.nop(), inc=s_mr2f)
            wt = [(s_vap2, s_vap2.n)]
            if gg >= 2 and dc % 2 == 0:
                wt.append((s_h2w, h2w_cnt[gg - 2]))
            if dc >= 2:
                wt.append((s_gmm, s_gmm.n - 1))
            p.op("scalar", lambda h, d=dc: h.activation(
                     h2f_t[:, d % 2, :], h1f[:, d % 2, :], AF.Identity,
                     bias=sbv[:, d:d + 1], scale=oncb[:, d:d + 1]),
                 waits=wt, inc=s_h2f)
            wt = [(s_h2f, s_h2f.n)]
            if dc == 0:
                wt.append((s_bcv, s_bcv.n))
                if q >= 1:
                    wt.append((s_gbf, q))
            p.op("tensor", lambda h, d=dc: h.matmul(
                     ps[0:8, 4, 0:TQ], gateT_sb[:, d, :],
                     h2f_t[:, d % 2, :],
                     start=(d == 0), stop=(d == DC - 1)),
                 waits=wt, inc=s_gmm)
            p.op("scalar", lambda h, d=dc, b=gbuf: h.copy(
                     h2g[:, b, d % 2, :], h2f_t[:, d % 2, :]), inc=s_h2v)
            if dc % 2 == 1:
                nw = p.op("sync", lambda h, qq=q, g=gi, b=gbuf: h.dma_start(
                              out=h2_sp[:, g * 2:(g + 1) * 2,
                                        qq * TQ:(qq + 1) * TQ],
                              in_=h2g[:, b, :, :]),
                          waits=[(s_h2v, s_h2v.n)], inc=s_h2w, inc_k=16)
                h2w_cnt.append(nw)
        p.op("scalar", lambda h: h.activation(
                 sc8[:], ps[0:8, 4, 0:TQ], AF.Sigmoid),
             waits=[(s_gmm, s_gmm.n), (s_trin, 4 * q)], inc=s_gsig)
        p.op("scalar", lambda h: h.nop(), inc=s_gbf)
        for tt in range(4):
            wt = [(s_gsig, s_gsig.n)] if tt == 0 else []
            if 4 * q + tt >= 2:
                wt.append((s_trv, 4 * q + tt - 1))
            p.op("tensor", lambda h, t=tt: h.transpose(
                     ps[0:128, 2 + t % 2, 0:8],
                     sc8[:, t * 128:(t + 1) * 128], eye_sb[0:8, 0:8]),
                 waits=wt, inc=s_trin)
            p.op("vector", lambda h, qq=q, t=tt: h.tensor_copy(
                     sc_sb[:, qq * 4 + t, :], ps[0:128, 2 + t % 2, 0:8]),
                 waits=[(s_trin, s_trin.n)], inc=s_trv)
    def bounce():
        p.op("scalar", lambda h: h.nop(), waits=[(s_ka, s_ka.n)], inc=s_kb)
        p.wait("vector", s_kb, s_kb.n)
    def stage(fn):
        for tt in range(NT128):
            p.op("vector", lambda h, t=tt: fn(h, t),
                 inc=(s_ka if tt == NT128 - 1 else None))
        bounce()
    for tt in range(NT128):
        p.op("vector", lambda h, t=tt: h.reduce_max(
                 mk[:, t, 0, 0:1], sc_sb[:, t, :], axis=AXX),
             waits=([(s_trv, s_trv.n)] if tt == 0 else []),
             inc=(s_ka if tt == NT128 - 1 else None))
    bounce()
    stage(lambda h, t: h.tensor_scalar(
              mk[:, t, 1, :], sc_sb[:, t, :], mk[:, t, 0, 0:1], None,
              op0=ALU.is_equal))
    stage(lambda h, t: h.tensor_tensor_scan(
              mk[:, t, 2, :], ones8[:], mk[:, t, 1, :], 0.0,
              op0=ALU.mult, op1=ALU.add))
    stage(lambda h, t: h.tensor_scalar(
              mk[:, t, 2, :], mk[:, t, 2, :], 1.0, None, op0=ALU.is_equal))
    stage(lambda h, t: h.tensor_mul(
              mk[:, t, 2, :], mk[:, t, 2, :], mk[:, t, 1, :]))
    stage(lambda h, t: h.scalar_tensor_tensor(
              mk[:, t, 3, :], mk[:, t, 2, :], -30.0, sc_sb[:, t, :],
              op0=ALU.mult, op1=ALU.add))
    stage(lambda h, t: h.reduce_max(
              mk[:, t, 0, 1:2], mk[:, t, 3, :], axis=AXX))
    stage(lambda h, t: h.tensor_scalar(
              mk[:, t, 4, :], mk[:, t, 3, :], mk[:, t, 0, 1:2], None,
              op0=ALU.is_equal))
    stage(lambda h, t: h.tensor_tensor_scan(
              mk[:, t, 5, :], ones8[:], mk[:, t, 4, :], 0.0,
              op0=ALU.mult, op1=ALU.add))
    stage(lambda h, t: h.tensor_scalar(
              mk[:, t, 5, :], mk[:, t, 5, :], 1.0, None, op0=ALU.is_equal))
    stage(lambda h, t: h.tensor_mul(
              mk[:, t, 5, :], mk[:, t, 5, :], mk[:, t, 4, :]))
    stage(lambda h, t: h.tensor_scalar_mul(
              mk[:, t, 1, :], mk[:, t, 2, :], mk[:, t, 0, 0:1]))
    for tt in range(NT128):
        p.op("vector", lambda h, t=tt: h.scalar_tensor_tensor(
                 m_sb[:, t, :], mk[:, t, 5, :], mk[:, t, 0, 1:2],
                 mk[:, t, 1, :], op0=ALU.mult, op1=ALU.add),
             inc=(s_msk if tt == NT128 - 1 else None))

    p.op("vector", lambda h: h.memset(mT_sb[:, :], 1.0))
    for tt in range(NT128):
        bk = 5 + 2 * (tt % 2)
        wt = [(s_msk, s_msk.n)]
        if tt < 2:
            wt.append((s_st2f, nq))
            wt.append((s_gbf, nq))
        else:
            wt.append((s_trpf, tt - 1))
        p.op("tensor", lambda h, t=tt, k=bk: h.transpose(
                 ps[0:8, k, 0:128], m_sb[:, t, :], eye_sb[:]),
             waits=wt, inc=s_trp)
        p.op("vector", lambda h, t=tt, k=bk: h.tensor_copy(
                 mT_sb[0:8, t * 128:(t + 1) * 128], ps[0:8, k, 0:128]),
             waits=[(s_trp, s_trp.n)], inc=s_trpf)
    nmtw = p.op("sync", lambda h: h.dma_start(out=mT_hbm[:, 0:Tn], in_=mT_sb[:]),
                waits=[(s_trpf, NT128)], inc=s_mtw, inc_k=16)

    p.wait("sync", s_x2w, x2w_cnt[-1])
    p.wait("sync", s_h2w, h2w_cnt[-1])
    ph1.close()
    p.barrier("b_ph1")

    ph2a = ES()
    sba = lambda *a: ph2a.enter_context(nc.sbuf_tensor(*a))
    h2_sb = sba("h2_sb", [128, DC, Tn], BF16)
    wslots = sba("wslots", [128, 3, DC, H], BF16)
    mrep = sba("mrep", [128, 2, Tn], BF16)
    t1sb = sba("t1sb", [128, nq, TQ], F32)
    gbuf = sba("gbuf", [128, 2, nq, TQ], BF16)

    s_h2ld = S("s_h2ld")
    s_w = S("s_w"); s_wf = S("s_wf")
    s_mrep = S("s_mrep"); s_mrepf = S("s_mrepf")
    s_u1mm = S("s_u1mm"); s_t1v = S("s_t1v"); s_silu = S("s_silu")
    s_u3mm = S("s_u3mm"); s_gv = S("s_gv"); s_gw = S("s_gw")

    nh2 = p.op("sync", lambda h: h.dma_start(out=h2_sb[:], in_=h2_sp),
               waits=[(s_h2w, h2w_cnt[-1])], inc=s_h2ld, inc_k=16)

    w_cnt = []
    mrep_cnt = []
    gw_cnt = []
    for e in range(NE):
        for (k, W) in ((2 * e, W1), (2 * e + 1, W3)):
            slot = k % 3
            wt = [(s_wf, k - 2)] if k >= 3 else []
            nwk = p.op("gpsimd", lambda h, s=slot, ee=e, WW=W: h.dma_start(
                           out=wslots[:, s, :, :],
                           in_=WW[ee].rearrange("(c p) hh -> p c hh", p=128)),
                       waits=wt, inc=s_w, inc_k=16)
            w_cnt.append(nwk)
        wt = [(s_mtw, nmtw)]
        if e >= 2:
            wt.append((s_mrepf, e - 1))
        nm = p.op("gpsimd", lambda h, ee=e: h.dma_start(
                      out=mrep[:, ee % 2, :],
                      in_=mT_hbm[ee:ee + 1, 0:Tn].partition_broadcast(128)),
                  waits=wt, inc=s_mrep, inc_k=16)
        mrep_cnt.append(nm)

    for e in range(NE):
        s1, s3 = (2 * e) % 3, (2 * e + 1) % 3
        for hc in range(HC):
            idx = e * HC + hc
            for dc in range(DC):
                wt = []
                if dc == 0:
                    wt.append((s_w, w_cnt[2 * e]))
                    if idx >= 1:
                        wt.append((s_t1v, idx))
                    if idx == 0:
                        wt.append((s_h2ld, nh2))
                for tt in range(nq):
                    p.op("tensor", lambda h, d=dc, t=tt, c=hc, s=s1: h.matmul(
                             bank(t), wslots[:, s, d, c * 128:(c + 1) * 128],
                             h2_sb[:, d, t * TQ:(t + 1) * TQ],
                             start=(d == 0), stop=(d == DC - 1)),
                         waits=(wt if tt == 0 else []),
                         inc=(s_u1mm if dc == DC - 1 and tt == nq - 1 else None))
            if hc == HC - 1:
                p.op("tensor", lambda h: h.nop(), inc=s_wf)
            for dc in range(DC):
                wt = []
                if dc == 0:
                    wt.append((s_w, w_cnt[2 * e + 1]))
                    if idx >= 1:
                        wt.append((s_gv, idx))
                for tt in range(nq):
                    p.op("tensor", lambda h, d=dc, t=tt, c=hc, s=s3: h.matmul(
                             bank(4 + t), wslots[:, s, d, c * 128:(c + 1) * 128],
                             h2_sb[:, d, t * TQ:(t + 1) * TQ],
                             start=(d == 0), stop=(d == DC - 1)),
                         waits=(wt if tt == 0 else []),
                         inc=(s_u3mm if dc == DC - 1 and tt == nq - 1 else None))
            if hc == HC - 1:
                p.op("tensor", lambda h: h.nop(), inc=s_wf)
            for tt in range(nq):
                wt = []
                if tt == 0:
                    wt = [(s_u1mm, s_u1mm.n), (s_mrep, mrep_cnt[e])]
                    if idx >= 1:
                        wt.append((s_silu, idx))
                p.op("vector", lambda h, t=tt, ee=e: h.tensor_mul(
                         t1sb[:, t, :], bank(t),
                         mrep[:, ee % 2, t * TQ:(t + 1) * TQ]),
                     waits=wt, inc=(s_t1v if tt == nq - 1 else None))
            for tt in range(nq):
                p.op("scalar", lambda h, t=tt: h.activation(
                         t1sb[:, t, :], t1sb[:, t, :], AF.Silu),
                     waits=([(s_t1v, s_t1v.n)] if tt == 0 else []),
                     inc=(s_silu if tt == nq - 1 else None))
            for tt in range(nq):
                wt = []
                if tt == 0:
                    wt = [(s_u3mm, s_u3mm.n), (s_silu, s_silu.n)]
                    if idx >= 2:
                        wt.append((s_gw, gw_cnt[idx - 2]))
                p.op("vector", lambda h, t=tt: h.tensor_mul(
                         t1sb[:, t, :], t1sb[:, t, :], bank(4 + t)), waits=wt)
                p.op("vector", lambda h, t=tt, b=hc % 2, ee=e: h.tensor_mul(
                         gbuf[:, b, t, :], t1sb[:, t, :],
                         mrep[:, ee % 2, t * TQ:(t + 1) * TQ]),
                     inc=(s_gv if tt == nq - 1 else None))
            if hc == HC - 1:
                p.op("vector", lambda h: h.nop(), inc=s_mrepf)
            ng = p.op("sync", lambda h, ee=e, c=hc, b=hc % 2: h.dma_start(
                          out=g_sp[ee, c],
                          in_=gbuf[:, b, :, :].rearrange("p a t -> p (a t)")),
                      waits=[(s_gv, s_gv.n)], inc=s_gw, inc_k=16)
            gw_cnt.append(ng)

    p.wait("sync", s_gw, gw_cnt[-1])
    ph2a.close()
    p.barrier("b_ph2a")

    ph2b = ES()
    sbb = lambda *a: ph2b.enter_context(nc.sbuf_tensor(*a))
    gpan = sbb("gpan", [128, NE, HC, TQ], BF16)
    w2t = sbb("w2t", [128, 2, NE, HC, 128], BF16)
    x2b = sbb("x2b", [128, 2, TQ], F32)
    o1 = sbb("o1", [128, 2, TQ], F32)
    o2 = sbb("o2", [128, 2, TQ], F32)

    s_gpld = S("s_gpld"); s_gpf = S("s_gpf")
    s_w2 = S("s_w2"); s_w2f = S("s_w2f")
    s_omm = S("s_omm"); s_oev = S("s_oev")
    s_x2b = S("s_x2b"); s_x2bf = S("s_x2bf")
    s_ov = S("s_ov"); s_out = S("s_out")

    n_out = 0
    w2_n = {}
    x2_n = {}

    def emit_loads(gi):
        tp, dc = divmod(gi, DC)
        par = gi % 2
        wt = [(s_w2f, gi - 1)] if gi >= 2 else []
        if gi == 0:
            wt.append((s_w2c, s_w2c.n))
        w2_n[gi] = p.op("sync", lambda h, d=dc, b=par: h.dma_start(
                            out=w2t[:, b, :, :, :],
                            in_=W2b[d].rearrange("e c p j -> p e c j")),
                        waits=wt, inc=s_w2, inc_k=16)
        wt = [(s_x2bf, gi - 1)] if gi >= 2 else [(s_x2w, x2w_cnt[-1])]
        x2_n[gi] = p.op("sync", lambda h, d=dc, t=tp, b=par: h.dma_start(
                            out=x2b[:, b, :],
                            in_=x2_sp[:, d, t * TQ:(t + 1) * TQ]),
                        waits=wt, inc=s_x2b, inc_k=16)

    for tp in range(nq):
        gp_n = []
        for e in range(NE):
            wt = []
            if e == 0:
                wt = [(s_gw, gw_cnt[-1])] if tp == 0 else [(s_gpf, tp)]
            n = p.op("sync", lambda h, t=tp, ee=e: h.dma_start(
                         out=gpan[:, ee, :, :],
                         in_=g_sp[ee, :, :, t * TQ:(t + 1) * TQ]
                         .rearrange("c p t -> p c t")),
                     waits=wt, inc=s_gpld, inc_k=16)
            gp_n.append(n)
        for dc in range(DC):
            gi = tp * DC + dc
            par = gi % 2
            if dc == 0:
                emit_loads(gi)
            if gi + 1 < nq * DC and (gi + 1) % DC != 0:
                emit_loads(gi + 1)
            first = True
            for e in range(NE):
                for hcc in range(HC):
                    wt = []
                    if first:
                        wt = [(s_w2, w2_n[gi])]
                        if gi >= 2:
                            wt.append((s_oev, gi - 1))
                    if hcc == 0 and gi % DC == 0:
                        wt.append((s_gpld, gp_n[e]))
                    last = (e == NE - 1 and hcc == HC - 1)
                    p.op("tensor", lambda h, ee=e, c=hcc, b=par, f=first:
                             h.matmul(bank(b), w2t[:, b, ee, c, :],
                                      gpan[:, ee, c, :],
                                      start=f, stop=(ee == NE - 1 and c == HC - 1)),
                         waits=wt, inc=(s_omm if last else None))
                    first = False
            p.op("tensor", lambda h: h.nop(), inc=s_w2f)
            if dc == DC - 1:
                p.op("tensor", lambda h: h.nop(), inc=s_gpf)
            wt = [(s_omm, s_omm.n)]
            if gi >= 2:
                wt.append((s_ov, gi - 1))
            p.op("scalar", lambda h, d=dc, b=par: h.activation(
                     o1[:, b, :], bank(b), AF.Identity, scale=gbc[:, d:d + 1]),
                 waits=wt, inc=s_oev)
            wt = [(s_oev, s_oev.n), (s_x2b, x2_n[gi])]
            if gi >= 2:
                wt.append((s_out, 16 * (gi - 1)))
            p.op("vector", lambda h, b=par: h.tensor_add(
                     o2[:, b, :], o1[:, b, :], x2b[:, b, :]),
                 waits=wt, inc=s_ov)
            p.op("vector", lambda h: h.nop(), inc=s_x2bf)
            n_out = p.op("sync", lambda h, d=dc, t=tp, b=par: h.dma_start(
                             out=outT[d * 128:(d + 1) * 128,
                                      t * TQ:(t + 1) * TQ],
                             in_=o2[:, b, :]),
                         waits=[(s_ov, s_ov.n)], inc=s_out, inc_k=16)

    p.wait("sync", s_out, n_out)
    import kernel as _k
    _k._LAST_PROG = p
    counts = p.replay()
    nc.finalize()
    ph2b.close()
    stack.close()
    return nc, counts


def host_prep(inp, nq=4):
    Tn = nq * TQ
    f = np.float32
    shared = {}
    shared["wmod"] = np.ascontiguousarray(np.asarray(inp["L0_wmod"], f))
    shared["gateT"] = np.ascontiguousarray(np.asarray(inp["L0_ffn_gate_w"], f).T)
    shared["xprojT"] = np.ascontiguousarray(
        np.asarray(inp["L0_attn_xproj_w"], f).T)
    shared["dtprojT"] = np.ascontiguousarray(
        np.asarray(inp["L0_attn_dtproj_w"], f).T)
    shared["dtb"] = np.ascontiguousarray(
        np.asarray(inp["L0_attn_dtproj_b"], f).reshape(DC, 128).T)
    shared["convw"] = np.ascontiguousarray(
        np.asarray(inp["L0_attn_conv_w"], f).reshape(D, 4)
        .reshape(DC, 128, 4).transpose(1, 0, 2))
    shared["convb"] = np.ascontiguousarray(
        np.asarray(inp["L0_attn_conv_b"], f).reshape(DC, 128).T)
    shared["alog"] = np.ascontiguousarray(
        np.asarray(inp["L0_attn_A_log"], f).reshape(DC, 128, NST)
        .transpose(1, 0, 2))
    shared["dskip"] = np.ascontiguousarray(
        np.asarray(inp["L0_attn_D_skip"], f).reshape(DC, 128).T)
    w1 = np.asarray(inp["L0_ffn_w1"], f)
    w3 = np.asarray(inp["L0_ffn_w3"], f)
    w2 = np.asarray(inp["L0_ffn_w2"], f)
    ws1 = np.asarray(inp["L0_ffn_ws1"], f)
    ws3 = np.asarray(inp["L0_ffn_ws3"], f)
    ws2 = np.asarray(inp["L0_ffn_ws2"], f)
    W1 = np.concatenate([w1, ws1[None]], 0).transpose(0, 2, 1)
    W3 = np.concatenate([w3, ws3[None]], 0).transpose(0, 2, 1)
    W2T = np.concatenate([w2, ws2[None]], 0).transpose(0, 2, 1)
    shared["W1"] = np.ascontiguousarray(W1)
    shared["W3"] = np.ascontiguousarray(W3)
    shared["W2s"] = np.ascontiguousarray(
        W2T.reshape(NE, HC, 128, DC, 128).transpose(3, 0, 1, 2, 4))
    shared["eye"] = np.eye(128, dtype=f)
    shared["onesv"] = np.ones((128, 1), dtype=f)

    x = np.asarray(inp["L0_attn_x"], f)
    c = np.asarray(inp["L0_c"], f)
    per_core = []
    for i in range(8):
        m = {}
        m["xT"] = np.ascontiguousarray(x[i, :Tn, :].T)
        m["c16"] = np.ascontiguousarray(c[i].reshape(DC, 128).T)
        per_core.append(m)
    return shared, per_core


def kernel(**inp):
    nq = 4
    key = ("prog", nq)
    if key not in _CACHE:
        _CACHE[key] = build(nq)
    nc, _ = _CACHE[key]
    shared, per_core = host_prep(inp, nq)
    in_maps = [dict(shared, **pc) for pc in per_core]
    res = run_bass_kernel_spmd(nc, in_maps, core_ids=list(range(8)))
    out = np.empty((8, nq * TQ, D), dtype=np.float32)
    for i in range(8):
        out[i] = np.asarray(res.results[i]["outT"]).T
    return out


# version 18
# speedup vs baseline: 1.0623x; 1.0623x over previous
import numpy as np
import ml_dtypes
import concourse.bass as bass
import concourse.mybir as mybir
from concourse.bass_utils import run_bass_kernel_spmd

F32 = mybir.dt.float32
BF16 = mybir.dt.bfloat16
AF = mybir.ActivationFunctionType
ALU = mybir.AluOpType
AXX = mybir.AxisListType.X

D = 2048
DC = 16
TQ = 512
NST = 16
H = 1024
HC = 8
NE = 9
EPS = 1e-5

_CACHE = {}


class Sem:
    def __init__(self, stack, nc, name):
        self.h = stack.enter_context(nc.semaphore(name))
        self.n = 0

    def inc(self, k=1):
        self.n += k
        return self.n


class Prog:
    ENGS = ("sync", "gpsimd", "tensor", "vector", "scalar")

    def __init__(self, nc, stack):
        self.nc = nc
        self.stack = stack
        self.q = {e: [] for e in self.ENGS}
        self._sems = {}

    def sem(self, name):
        if name not in self._sems:
            self._sems[name] = Sem(self.stack, self.nc, name)
        return self._sems[name]

    def op(self, eng, fn, waits=(), inc=None, inc_k=1):
        w = [(s.h, v) for (s, v) in waits if v > 0]
        n = None
        if inc is not None:
            n = inc.inc(inc_k)
        self.q[eng].append((w, fn, inc.h if inc else None, inc_k))
        return n

    def wait(self, eng, sem, val):
        self.q[eng].append(([(sem.h, val)], None, None, 0))

    def barrier(self, name):
        s = self.sem(name)
        for e in self.ENGS:
            self.op(e, lambda h: h.nop(), inc=s)
        tgt = s.n
        for e in self.ENGS:
            self.wait(e, s, tgt)

    def replay(self):
        counts = {e: len(self.q[e]) for e in self.ENGS}
        with self.nc.Block() as block:
            def runner(eng_name):
                def run(h):
                    for (w, fn, inc_h, inc_k) in self.q[eng_name]:
                        for (sh, v) in w:
                            h.wait_ge(sh, v)
                        if fn is not None:
                            ins = fn(h)
                            if inc_h is not None:
                                ins.then_inc(inc_h, inc_k)
                        elif inc_h is not None:
                            h.nop().then_inc(inc_h, inc_k)
                return run
            block.sync(runner("sync"))
            block.gpsimd(runner("gpsimd"))
            block.tensor(runner("tensor"))
            block.vector(runner("vector"))
            block.scalar(runner("scalar"))
        return counts


def build(nq=4, debug_spills=False):
    from contextlib import ExitStack
    nc = bass.Bass("TRN2", target_bir_lowering=False, debug=False,
                   detect_race_conditions=False)
    stack = ExitStack()
    Tn = nq * TQ
    NT128 = Tn // 128

    din = {}
    def dram_in(name, shape, dtype=F32):
        din[name] = nc.dram_tensor(name, list(shape), dtype,
                                   kind="ExternalInput").ap()
        return din[name]

    xT = dram_in("xT", (D, Tn))
    wmod = dram_in("wmod", (D, 6 * D))
    c16 = dram_in("c16", (128, DC))
    gateT = dram_in("gateT", (D, 8))
    xprojT = dram_in("xprojT", (D, 160))
    dtprojT = dram_in("dtprojT", (128, D))
    dtb = dram_in("dtb", (128, DC))
    convw = dram_in("convw", (128, DC, 4))
    convb = dram_in("convb", (128, DC))
    alog = dram_in("alog", (128, DC, NST))
    dskip = dram_in("dskip", (128, DC))
    W1 = dram_in("W1", (NE, D, H))
    W3 = dram_in("W3", (NE, D, H))
    W2s = dram_in("W2s", (DC, NE, HC, 128, 128))
    eye = dram_in("eye", (128, 128))
    onesv = dram_in("onesv", (128, 1))

    outT = nc.dram_tensor("outT", [D, Tn], F32, kind="ExternalOutput").ap()
    stat_hbm = nc.dram_tensor("stat_hbm", [2, TQ], F32, kind="Internal").ap()
    bc_hbm = nc.dram_tensor("bc_hbm", [32, TQ], F32, kind="Internal").ap()
    mT_hbm = nc.dram_tensor("mT_hbm", [NE, 8 * TQ], BF16,
                            kind=("ExternalOutput" if debug_spills else "Internal")).ap()
    if debug_spills:
        dbg_m = nc.dram_tensor("dbg_m", [128, Tn // 128, 8], F32,
                               kind="ExternalOutput").ap()
        dbg_sc8 = nc.dram_tensor("dbg_sc8", [8, TQ], F32,
                                 kind="ExternalOutput").ap()
    dbg_kind = "ExternalOutput" if debug_spills else "Internal"
    x2_sp = nc.dram_tensor("x2_sp", [128, DC, Tn], F32, kind=dbg_kind).ap()
    h2_sp = nc.dram_tensor("h2_sp", [128, DC, Tn], BF16, kind=dbg_kind).ap()
    g_sp = nc.dram_tensor("g_sp", [NE, HC, 128, Tn], BF16, kind=dbg_kind).ap()
    W2b = nc.dram_tensor("W2b", [DC, NE, HC, 128, 128], BF16, kind="Internal").ap()

    p = Prog(nc, stack)
    S = p.sem

    sb = lambda *a: stack.enter_context(nc.sbuf_tensor(*a))
    ones_sb = sb("ones_sb", [128, 1], F32)
    eye_sb = sb("eye_sb", [128, 128], F32)
    c_sb = sb("c_sb", [128, DC], F32)
    silc_sb = sb("silc_sb", [128, DC], F32)
    gateT_sb = sb("gateT_sb", [128, DC, 8], F32)
    dtb_sb = sb("dtb_sb", [128, DC], F32)
    convw_sb = sb("convw_sb", [128, DC, 4], F32)
    convb_sb = sb("convb_sb", [128, DC], F32)
    alog_sb = sb("alog_sb", [128, DC, NST], F32)
    A_sb = alog_sb
    dskip_sb = sb("dskip_sb", [128, DC], F32)
    modv = sb("modv", [128, 6, DC], F32)
    onca = sb("onca", [128, DC], F32)
    oncb = sb("oncb", [128, DC], F32)
    gac = sb("gac", [128, DC], F32)
    gbc = sb("gbc", [128, DC], F32)
    sav = sb("sav", [128, DC], F32)
    sbv = sb("sbv", [128, DC], F32)
    m_sb = sb("m_sb", [128, NT128, 8], F32)
    halo = sb("halo", [128, DC, 3], F32)
    carry = sb("carry", [128, DC * NST], F32)
    kconst = sb("kconst", [128, 4], F32)
    ones8 = sb("ones8", [128, 8], F32)
    onesb = sb("onesb", [128, 1], BF16)
    ps = stack.enter_context(nc.psum_tensor("ps", [128, 8, 512], F32))

    def bank(b, m=128, n=512):
        return ps[0:m, b, 0:n]

    s_c = S("s_c"); s_cg = S("s_cg"); s_sil = S("s_sil")
    s_wm = S("s_wm"); s_wmf = S("s_wmf")
    s_gemv = S("s_gemv"); s_gemvf = S("s_gemvf")
    s_mrow = S("s_mrow"); s_modv = S("s_modv"); s_modr = S("s_modr")

    def ld(eng, dst, src, sem):
        return p.op(eng, lambda h, d=dst, s=src: h.dma_start(out=d, in_=s),
                    inc=sem, inc_k=16)

    ld("sync", ones_sb[:], onesv, s_c)
    ld("sync", eye_sb[:], eye, s_c)
    ld("sync", c_sb[:], c16, s_c)
    ld("sync", gateT_sb[:], gateT.rearrange("(c p) e -> p c e", p=128), s_c)
    ld("sync", dtb_sb[:], dtb, s_c)
    ld("sync", convw_sb[:], convw, s_c)
    ld("sync", convb_sb[:], convb, s_c)
    ld("sync", alog_sb[:], alog, s_c)
    ld("sync", dskip_sb[:], dskip, s_c)
    n_c = s_c.n

    p.op("scalar", lambda h: h.activation(silc_sb[:], c_sb[:], AF.Silu),
         waits=[(s_c, n_c)], inc=s_sil)
    p.op("scalar", lambda h: h.activation(A_sb[:], alog_sb[:], AF.Exp), inc=s_sil)
    p.op("vector", lambda h: h.tensor_scalar_mul(A_sb[:], A_sb[:], -1.0),
         waits=[(s_sil, 2), (s_c, n_c)], inc=s_modr)
    p.op("vector", lambda h: h.memset(halo[:], 0.0), inc=s_modr)
    p.op("vector", lambda h: h.memset(carry[:], 0.0), inc=s_modr)
    p.op("vector", lambda h: h.memset(kconst[:, 0:1], EPS), inc=s_modr)
    p.op("vector", lambda h: h.memset(kconst[:, 1:2], -0.5), inc=s_modr)
    p.op("vector", lambda h: h.memset(kconst[:, 2:3], -1.0), inc=s_modr)
    p.op("vector", lambda h: h.memset(kconst[:, 3:4], 1.0), inc=s_modr)
    p.op("vector", lambda h: h.memset(ones8[:], 1.0), inc=s_modr)
    p.op("vector", lambda h: h.memset(onesb[:], 1.0), inc=s_modr)

    with nc.sbuf_tensor("wmbuf", [128, 2, DC, 512], F32) as wmbuf, \
         nc.sbuf_tensor("modrow", [1, 6 * D], F32) as modrow:
        njt = 6 * D // 512
        for jt in range(njt):
            buf = jt % 2
            w = [(s_wmf, 16 * (jt - 1))] if jt >= 2 else []
            ld_n = p.op("sync",
                        lambda h, b=buf, j=jt: h.dma_start(
                            out=wmbuf[:, b, :, :],
                            in_=wmod[:, j * 512:(j + 1) * 512]
                            .rearrange("(c p) j -> p c j", p=128)),
                        waits=w, inc=s_wm, inc_k=16)
            bk = 6 + jt % 2
            for dc in range(DC):
                wt = []
                if dc == 0:
                    wt = [(s_wm, ld_n), (s_sil, 1)]
                    if jt >= 2:
                        wt.append((s_gemvf, jt - 1))
                p.op("tensor",
                     lambda h, b=buf, d=dc, k=bk: h.matmul(
                         bank(k, 1), silc_sb[:, d:d + 1], wmbuf[:, b, d, :],
                         start=(d == 0), stop=(d == DC - 1)),
                     waits=wt, inc=(s_gemv if dc == DC - 1 else None))
            p.op("vector",
                 lambda h, j=jt, k=bk: h.tensor_copy(
                     modrow[:, j * 512:(j + 1) * 512], bank(k, 1)),
                 waits=[(s_gemv, jt + 1)], inc=s_gemvf)
            p.op("vector", lambda h: h.nop(), inc=s_wmf, inc_k=16)
        s_mtr = S("s_mtr"); s_mtrv = S("s_mtrv")
        for v in range(6):
            bk = 5 + 2 * (v % 2)
            for cc in range(DC):
                gj = v * DC + cc
                wt = []
                if gj == 0:
                    wt.append((s_gemvf, njt))
                    wt.append((s_c, n_c))
                if v >= 2 and cc == 0:
                    wt.append((s_mtrv, v - 1))
                p.op("tensor", lambda h, k=bk, j=gj, c=cc: h.transpose(
                         ps[0:128, k, c:c + 1],
                         modrow[:, j * 128:(j + 1) * 128],
                         eye_sb[0:1, 0:1]),
                     waits=wt, inc=(s_mtr if cc == DC - 1 else None))
            p.op("vector", lambda h, k=bk, vv=v: h.tensor_copy(
                     modv[:, vv, :], ps[0:128, k, 0:DC]),
                 waits=[(s_mtr, s_mtr.n)], inc=s_mtrv)
        n_ev = s_mtrv.n

    def drv(fn):
        p.op("vector", fn, waits=[(s_mtrv, n_ev)], inc=s_modr)
    drv(lambda h: h.tensor_copy(sav[:], modv[:, 0, :]))
    drv(lambda h: h.tensor_scalar_add(onca[:], modv[:, 1, :], 1.0))
    drv(lambda h: h.tensor_copy(sbv[:], modv[:, 3, :]))
    drv(lambda h: h.tensor_scalar_add(oncb[:], modv[:, 4, :], 1.0))
    with nc.sbuf_tensor("clt", [128, 3, DC], F32) as clt:
        for (src_i, dst) in ((2, gac), (5, gbc)):
            drv(lambda h, i=src_i: h.tensor_scalar_max(
                    clt[:, 0, :], modv[:, i, :], 0.25))
            drv(lambda h, i=src_i: h.tensor_scalar_min(
                    clt[:, 1, :], modv[:, i, :], -0.25))
            drv(lambda h, i=src_i: h.tensor_scalar(
                    clt[:, 2, :], modv[:, i, :], 0.0, None, op0=ALU.is_ge))
            drv(lambda h: h.tensor_sub(clt[:, 0, :], clt[:, 0, :], clt[:, 1, :]))
            drv(lambda h: h.tensor_mul(clt[:, 2, :], clt[:, 2, :], clt[:, 0, :]))
            drv(lambda h, d=dst: h.tensor_add(d[:], clt[:, 1, :], clt[:, 2, :]))
    n_modr = s_modr.n

    p.barrier("b_ph0")

    s_w2c = S("s_w2c")
    for dc in range(DC):
        p.op("gpsimd", lambda h, d=dc: h.dma_start(out=W2b[d], in_=W2s[d]),
             inc=s_w2c, inc_k=16)
    n_w2c = s_w2c.n

    from contextlib import ExitStack as ES
    ph1 = ES()
    sb1 = lambda *a: ph1.enter_context(nc.sbuf_tensor(*a))
    statb = sb1("statb", [1, 3, TQ], F32)
    xprojT_sb = sb1("xprojT_sb", [128, DC, 160], BF16)
    dtprojT_sb = sb1("dtprojT_sb", [128, DC, 128], BF16)
    mT_sb = sb1("mT_sb", [NE, Tn], BF16)
    x_q = sb1("x_q", [128, DC, TQ], F32)
    u_bf = sb1("u_bf", [128, DC, TQ], BF16)
    delta_q = sb1("delta_q", [128, DC, TQ], F32)
    du_q = sb1("du_q", [128, DC, TQ], BF16)
    y_q = sb1("y_q", [128, DC, TQ], F32)
    bcrep = sb1("bcrep", [128, 4, 2, TQ], BF16)
    dA_t = sb1("dA_t", [128, 2, TQ], F32)
    dBu_t = sb1("dBu_t", [128, TQ], BF16)
    h_t = sb1("h_t", [128, TQ], BF16)
    tmp_t = sb1("tmp_t", [128, TQ], BF16)
    mr1 = sb1("mr1", [128, 2, TQ], F32)
    mr2 = mr1
    h1x = sb1("h1x", [128, 2, TQ + 3], BF16)
    h1f = sb1("h1f", [128, 2, TQ], F32)
    sq_t = sb1("sq_t", [128, 2, TQ], BF16)
    u_t = sb1("u_t", [128, 2, TQ], F32)
    sp_t = u_t
    dtlow = sb1("dtlow", [128, TQ], BF16)
    bc_sb = sb1("bc_sb", [32, TQ], F32)
    x2g = sb1("x2g", [128, 2, 2, TQ], F32)
    x2in = x2g
    h2g = sb1("h2g", [128, 2, 2, TQ], BF16)
    h2f_t = sb1("h2f_t", [128, 2, TQ], F32)
    sc_sb = sb1("sc_sb", [128, NT128, 8], F32)
    sc8 = sb1("sc8", [8, TQ], F32)
    mk = sb1("mk", [128, NT128, 6, 8], F32)

    s_x = S("s_x"); s_xf = S("s_xf")
    s_sq1 = S("s_sq1"); s_smm1 = S("s_smm1")
    s_st1 = S("s_st1"); s_st1v = S("s_st1v"); s_st1f = S("s_st1f")
    s_mr1 = S("s_mr1"); s_mr1f = S("s_mr1f")
    s_vap = S("s_vap"); s_h1 = S("s_h1"); s_cv = S("s_cv"); s_ubf = S("s_ubf")
    s_xp = S("s_xp"); s_dtl = S("s_dtl"); s_bcv = S("s_bcv")
    s_dmm = S("s_dmm"); s_spf = S("s_spf")
    s_sp4 = S("s_sp4"); s_dlt = S("s_dlt")
    s_bcr = S("s_bcr"); s_bcrf = S("s_bcrf")
    s_dA = S("s_dA"); s_dAf = S("s_dAf")
    s_x2v = S("s_x2v"); s_x2w = S("s_x2w"); s_x2t = S("s_x2t")
    s_sq2 = S("s_sq2"); s_smm2 = S("s_smm2")
    s_st2 = S("s_st2"); s_st2v = S("s_st2v"); s_st2f = S("s_st2f")
    s_mr2 = S("s_mr2"); s_mr2f = S("s_mr2f")
    s_x2i = S("s_x2i"); s_x2if = S("s_x2if")
    s_vap2 = S("s_vap2"); s_h2f = S("s_h2f")
    s_gmm = S("s_gmm"); s_gsig = S("s_gsig"); s_gbf = S("s_gbf")
    s_trv = S("s_trv"); s_trin = S("s_trin")
    s_ka = S("s_ka"); s_kb = S("s_kb")
    s_h2v = S("s_h2v"); s_h2w = S("s_h2w")
    s_msk = S("s_msk"); s_trp = S("s_trp"); s_trpf = S("s_trpf")
    s_stw = S("s_stw"); s_bcw = S("s_bcw"); s_mtw = S("s_mtw")

    ld("gpsimd", xprojT_sb[:], xprojT.rearrange("(c p) e -> p c e", p=128), s_cg)
    ld("gpsimd", dtprojT_sb[:], dtprojT.rearrange("r (c j) -> r c j", c=DC), s_cg)
    n_cg = s_cg.n

    x2w_cnt = []
    h2w_cnt = []

    def fin_stats(stat, bsum, bsq, s_in, n_in, s_out, s_free, extra):
        p.op("vector", lambda h: h.tensor_scalar_mul(
                 stat[:, 0, :], bank(bsum, 1), 1.0 / D),
             waits=[(s_in, n_in)] + extra)
        p.op("vector", lambda h: h.tensor_scalar_mul(
                 stat[:, 1, :], bank(bsq, 1), 1.0 / D), inc=s_free)
        p.op("vector", lambda h: h.tensor_mul(
                 stat[:, 2, :], stat[:, 0, :], stat[:, 0, :]))
        p.op("vector", lambda h: h.tensor_sub(
                 stat[:, 1, :], stat[:, 1, :], stat[:, 2, :]), inc=s_out)
        p.op("scalar", lambda h: h.activation(
                 stat[:, 1, :], stat[:, 1, :], AF.Ln, bias=kconst[0:1, 0:1]),
             waits=[(s_out, s_out.n)])
        p.op("scalar", lambda h: h.activation(
                 stat[:, 1, :], stat[:, 1, :], AF.Exp,
                 scale=kconst[0:1, 1:2]), inc=s_out)

    for q in range(nq):
        wx = [(s_xf, q)] if q >= 1 else [(s_modr, n_modr)]
        nx = p.op("sync", lambda h, qq=q: h.dma_start(
                      out=x_q[:], in_=xT[:, qq * TQ:(qq + 1) * TQ]
                      .rearrange("(c p) t -> p c t", p=128)),
                  waits=wx, inc=s_x, inc_k=16)

        for dc in range(DC):
            wt = [(s_x, nx)] if dc == 0 else []
            if dc >= 2:
                wt.append((s_smm1, s_smm1.n - 1))
            elif q >= 1:
                wt.append((s_st2, q))
            p.op("scalar", lambda h, d=dc: h.activation(
                     sq_t[:, d % 2, :], x_q[:, d, :], AF.Square),
                 waits=wt, inc=s_sq1)
            wt = [(s_x, nx), (s_c, n_c)] if dc == 0 else []
            if dc == 0 and q >= 1:
                wt.append((s_st1f, q))
            p.op("tensor", lambda h, d=dc: h.matmul(
                     bank(0, 1), ones_sb[:], x_q[:, d, :],
                     start=(d == 0), stop=(d == DC - 1)), waits=wt)
            p.op("tensor", lambda h, d=dc: h.matmul(
                     bank(1, 1), onesb[:], sq_t[:, d % 2, :],
                     start=(d == 0), stop=(d == DC - 1)),
                 waits=[(s_sq1, s_sq1.n)],
                 inc=(s_st1 if dc == DC - 1 else s_smm1))
            if dc == DC - 1:
                s_smm1.inc(0)
        extra = [(s_stw, 32 * q)] if q >= 1 else []
        fin_stats(statb, 0, 1, s_st1, s_st1.n, s_st1v, s_st1f, extra)
        wsw = [(s_st1v, s_st1v.n), (s_mr2, 16 * q)]
        nsw1 = p.op("sync", lambda h: h.dma_start(
                        out=stat_hbm, in_=statb[0:1, 0:2, :]),
                    waits=wsw, inc=s_stw, inc_k=16)
        wmr = [(s_stw, nsw1)]
        if q >= 1:
            wmr.append((s_mr1f, q))
            wmr.append((s_mr2f, q))
        nmr1 = p.op("gpsimd", lambda h: h.dma_start(
                        out=mr1[:],
                        in_=stat_hbm.partition_broadcast(128)),
                    waits=wmr, inc=s_mr1, inc_k=16)

        for dc in range(DC):
            par = dc % 2
            wt = [(s_mr1, nmr1)] if dc == 0 else []
            if dc >= 2:
                wt.append((s_h1, s_h1.n - 1))
            p.op("vector", lambda h, d=dc, b=par: h.tensor_sub(
                     h1f[:, b, :], x_q[:, d, :], mr1[:, 0, :]), waits=wt)
            p.op("vector", lambda h, b=par: h.tensor_mul(
                     h1f[:, b, :], h1f[:, b, :], mr1[:, 1, :]),
                 inc=s_vap)
            if dc == DC - 1:
                p.op("vector", lambda h: h.nop(), inc=s_mr1f)
            p.op("scalar", lambda h, d=dc, b=par: h.activation(
                     h1x[:, b, 3:3 + TQ], h1f[:, b, :], AF.Identity,
                     bias=sav[:, d:d + 1], scale=onca[:, d:d + 1]),
                 waits=[(s_vap, s_vap.n)], inc=s_h1)
            p.op("vector", lambda h, d=dc, b=par: h.tensor_copy(
                     h1x[:, b, 0:3], halo[:, d, :]),
                 waits=[(s_h1, s_h1.n)])
            p.op("vector", lambda h, d=dc, b=par: h.tensor_copy(
                     halo[:, d, :], h1x[:, b, TQ:TQ + 3]))
            wt = [(s_ubf, s_ubf.n - 1)] if dc >= 2 else []
            p.op("vector", lambda h, d=dc, b=par: h.tensor_scalar(
                     u_t[:, d % 2, :], h1x[:, b, 3:3 + TQ],
                     convw_sb[:, d, 3:4], convb_sb[:, d:d + 1],
                     op0=ALU.mult, op1=ALU.add), waits=wt)
            for j in (2, 1, 0):
                p.op("vector", lambda h, d=dc, b=par, jj=j: h.scalar_tensor_tensor(
                         u_t[:, d % 2, :], h1x[:, b, jj:jj + TQ],
                         convw_sb[:, d, jj:jj + 1], u_t[:, d % 2, :],
                         op0=ALU.mult, op1=ALU.add),
                     inc=(s_cv if j == 0 else None))
            p.op("scalar", lambda h, d=dc: h.copy(u_bf[:, d, :], u_t[:, d % 2, :]),
                 waits=[(s_cv, s_cv.n)], inc=s_ubf)
            wt = [(s_ubf, s_ubf.n)]
            if dc == 0:
                if q >= 1:
                    wt.append((s_spf, 16 * q))
                    wt.append((s_trv, 4 * q))
                else:
                    wt.append((s_cg, n_cg))
            p.op("tensor", lambda h, d=dc: h.matmul(
                     bank(2), xprojT_sb[:, d, 0:128], u_bf[:, d, :],
                     start=(d == 0), stop=(d == DC - 1)), waits=wt)
            wt2 = []
            if dc == 0 and q >= 1:
                wt2.append((s_gbf, q))
            p.op("tensor", lambda h, d=dc: h.matmul(
                     bank(4, 32), xprojT_sb[:, d, 128:160], u_bf[:, d, :],
                     start=(d == 0), stop=(d == DC - 1)),
                 waits=wt2, inc=(s_xp if dc == DC - 1 else None))

        p.op("scalar", lambda h: h.copy(dtlow[:], bank(2)),
             waits=[(s_xp, s_xp.n), (s_dmm, s_dmm.n)], inc=s_dtl)
        p.op("vector", lambda h: h.tensor_copy(bc_sb[:], bank(4, 32)),
             waits=[(s_xp, s_xp.n), (s_bcw, 16 * q)], inc=s_bcv)
        nbw = p.op("sync", lambda h: h.dma_start(out=bc_hbm, in_=bc_sb[:]),
                   waits=[(s_bcv, s_bcv.n), (s_bcr, s_bcr.n)],
                   inc=s_bcw, inc_k=16)
        for dc in range(DC):
            bk = 2 + dc % 2
            wt = [(s_dtl, s_dtl.n)] if dc < 2 else [(s_spf, s_spf.n - 1)]
            if dc < 2 and q >= 1:
                wt.append((s_gbf, q))
                wt.append((s_trv, 4 * q))
            p.op("tensor", lambda h, d=dc, k=bk: h.matmul(
                     bank(k), dtprojT_sb[:, d, :], dtlow[:],
                     start=True, stop=True), waits=wt, inc=s_dmm)
            wsp = [(s_dmm, s_dmm.n)]
            if dc >= 1 or q >= 1:
                wsp.append((s_dlt, s_dlt.n))
            p.op("scalar", lambda h, d=dc, k=bk: h.activation(
                     sp_t[:, 0, :], bank(k), AF.Abs, bias=dtb_sb[:, d:d + 1]),
                 waits=wsp)
            p.op("scalar", lambda h, d=dc, k=bk: h.activation(
                     sp_t[:, 1, :], bank(k), AF.Relu, bias=dtb_sb[:, d:d + 1]),
                 inc=s_spf)
            p.op("scalar", lambda h: h.activation(
                     sp_t[:, 0, :], sp_t[:, 0, :], AF.Exp,
                     scale=kconst[:, 2:3]))
            p.op("scalar", lambda h: h.activation(
                     sp_t[:, 0, :], sp_t[:, 0, :], AF.Ln,
                     bias=kconst[:, 3:4]), inc=s_sp4)
            p.op("vector", lambda h, d=dc: h.tensor_add(
                     delta_q[:, d, :], sp_t[:, 1, :], sp_t[:, 0, :]),
                 waits=[(s_sp4, s_sp4.n)], inc=s_dlt)

        for dc in range(DC):
            wt = [(s_spf, s_spf.n - DC + dc + 1)]
            p.op("vector", lambda h, d=dc: h.tensor_mul(
                     du_q[:, d, :], delta_q[:, d, :], u_bf[:, d, :]), waits=wt)
            p.op("vector", lambda h, d=dc: h.tensor_scalar_mul(
                     y_q[:, d, :], u_bf[:, d, :], dskip_sb[:, d:d + 1]))
        it0 = DC * NST * q
        for n in range(NST):
            buf = n % 4
            gidx = q * NST + n
            wb = [(s_bcw, nbw)]
            if gidx >= 4:
                wb.append((s_bcrf, gidx - 3))
            p.op("gpsimd", lambda h, b=buf, nn=n: h.dma_start(
                     out=bcrep[:, b, 0, :],
                     in_=bc_hbm[nn:nn + 1, :].partition_broadcast(128)),
                 waits=wb, inc=s_bcr, inc_k=16)
            nb2 = p.op("gpsimd", lambda h, b=buf, nn=n: h.dma_start(
                           out=bcrep[:, b, 1, :],
                           in_=bc_hbm[16 + nn:17 + nn, :].partition_broadcast(128)),
                       inc=s_bcr, inc_k=16)
            for dc in range(DC):
                i = it0 + n * DC + dc
                dab = i % 2
                wda = [(s_dAf, i - 1)]
                if n == 0 and dc == 0:
                    wda.append((s_dlt, s_dlt.n))
                p.op("scalar", lambda h, d=dc, nn=n, b=dab: h.activation(
                         dA_t[:, b, :], delta_q[:, d, :], AF.Exp,
                         scale=A_sb[:, d, nn:nn + 1]),
                     waits=wda, inc=s_dA)
                wv = [(s_bcr, nb2)] if dc == 0 else []
                p.op("vector", lambda h, d=dc, b=buf: h.tensor_mul(
                         dBu_t[:], du_q[:, d, :], bcrep[:, b, 0, :]), waits=wv)
                ci = dc * NST + n
                p.op("vector", lambda h, b=dab, c=ci: h.tensor_tensor_scan(
                         h_t[:], dA_t[:, b, :], dBu_t[:], carry[:, c:c + 1],
                         op0=ALU.mult, op1=ALU.add),
                     waits=[(s_dA, s_dA.n)], inc=s_dAf)
                p.op("vector", lambda h, c=ci: h.tensor_copy(
                         carry[:, c:c + 1], h_t[:, TQ - 1:TQ]))
                p.op("vector", lambda h, b=buf: h.tensor_mul(
                         tmp_t[:], h_t[:], bcrep[:, b, 1, :]))
                p.op("vector", lambda h, d=dc: h.tensor_add(
                         y_q[:, d, :], y_q[:, d, :], tmp_t[:]),
                     inc=(s_bcrf if dc == DC - 1 else None))

        for dc in range(DC):
            gi = dc // 2
            gg = q * 8 + gi
            gbuf = gi % 2
            j = q * DC + dc
            wt = []
            if dc % 2 == 0 and gg >= 2:
                wt.append((s_x2w, x2w_cnt[gg - 2]))
            if j >= 4:
                wt.append((s_x2t, j - 3))
                wt.append((s_sq2, j - 3))
            if dc < 2:
                wt.append((s_st1, q + 1))
            p.op("vector", lambda h, d=dc, b=gbuf: h.scalar_tensor_tensor(
                     x2g[:, b, d % 2, :], y_q[:, d, :], gac[:, d:d + 1],
                     x_q[:, d, :], op0=ALU.mult, op1=ALU.add),
                 waits=wt, inc=s_x2v)
            if dc == DC - 1:
                p.op("vector", lambda h: h.nop(), inc=s_xf)
            wt = [(s_x2v, s_x2v.n)]
            if dc >= 2:
                wt.append((s_smm2, s_smm2.n - 1))
            else:
                wt.append((s_st1, q + 1))
            p.op("scalar", lambda h, d=dc, b=gbuf: h.activation(
                     sq_t[:, d % 2, :], x2g[:, b, d % 2, :], AF.Square),
                 waits=wt, inc=s_sq2)
            wt = [(s_x2v, s_x2v.n)]
            if dc == 0 and q >= 1:
                wt.append((s_st2f, q))
                wt.append((s_gbf, q))
            p.op("tensor", lambda h, d=dc, b=gbuf: h.matmul(
                     bank(5, 1), ones_sb[:], x2g[:, b, d % 2, :],
                     start=(d == 0), stop=(d == DC - 1)),
                 waits=wt, inc=(s_x2t if dc < DC - 1 else None))
            p.op("tensor", lambda h, d=dc: h.matmul(
                     bank(6, 1), onesb[:], sq_t[:, d % 2, :],
                     start=(d == 0), stop=(d == DC - 1)),
                 waits=[(s_sq2, s_sq2.n)],
                 inc=(s_st2 if dc == DC - 1 else s_smm2))
            if dc == DC - 1:
                p.op("tensor", lambda h: h.nop(), inc=s_x2t)
            if dc % 2 == 1:
                nw = p.op("sync", lambda h, qq=q, g=gi, b=gbuf: h.dma_start(
                              out=x2_sp[:, g * 2:(g + 1) * 2,
                                        qq * TQ:(qq + 1) * TQ],
                              in_=x2g[:, b, :, :]),
                          waits=[(s_x2v, s_x2v.n)], inc=s_x2w, inc_k=16)
                x2w_cnt.append(nw)
        extra = [(s_stw, 32 * q + 16)]
        fin_stats(statb, 5, 6, s_st2, s_st2.n, s_st2v, s_st2f, extra)
        wsw = [(s_st2v, s_st2v.n), (s_mr1, 16 * (q + 1))]
        nsw2 = p.op("sync", lambda h: h.dma_start(
                        out=stat_hbm, in_=statb[0:1, 0:2, :]),
                    waits=wsw, inc=s_stw, inc_k=16)
        wmr = [(s_stw, nsw2), (s_mr1f, q + 1)]
        if q >= 1:
            wmr.append((s_mr2f, q))
        nmr2 = p.op("gpsimd", lambda h: h.dma_start(
                        out=mr2[:],
                        in_=stat_hbm.partition_broadcast(128)),
                    waits=wmr, inc=s_mr2, inc_k=16)

        for dc in range(DC):
            gi = dc // 2
            gg = q * 8 + gi
            gbuf = gi % 2
            if dc % 2 == 0:
                wt = [(s_x2w, s_x2w.n),
                      (s_x2t, s_x2t.n)]
                if gg >= 2:
                    wt.append((s_x2if, gg - 1))
                p.op("sync", lambda h, qq=q, g=gi, b=gbuf: h.dma_start(
                         out=x2in[:, b, :, :],
                         in_=x2_sp[:, g * 2:(g + 1) * 2,
                                   qq * TQ:(qq + 1) * TQ]),
                     waits=wt, inc=s_x2i, inc_k=16)
            wt = [(s_x2i, s_x2i.n), (s_mr2, nmr2)] if dc % 2 == 0 else []
            if dc >= 2:
                wt.append((s_h2f, s_h2f.n - 1))
            p.op("vector", lambda h, d=dc, b=gbuf: h.tensor_sub(
                     h1f[:, d % 2, :], x2in[:, b, d % 2, :], mr2[:, 0, :]),
                 waits=wt, inc=(s_x2if if dc % 2 == 1 else None))
            p.op("vector", lambda h, d=dc: h.tensor_mul(
                     h1f[:, d % 2, :], h1f[:, d % 2, :], mr2[:, 1, :]),
                 inc=s_vap2)
            if dc == DC - 1:
                p.op("vector", lambda h: h.nop(), inc=s_mr2f)
            wt = [(s_vap2, s_vap2.n)]
            if gg >= 2 and dc % 2 == 0:
                wt.append((s_h2w, h2w_cnt[gg - 2]))
            if dc >= 2:
                wt.append((s_gmm, s_gmm.n - 1))
            p.op("scalar", lambda h, d=dc: h.activation(
                     h2f_t[:, d % 2, :], h1f[:, d % 2, :], AF.Identity,
                     bias=sbv[:, d:d + 1], scale=oncb[:, d:d + 1]),
                 waits=wt, inc=s_h2f)
            wt = [(s_h2f, s_h2f.n)]
            if dc == 0:
                wt.append((s_bcv, s_bcv.n))
                if q >= 1:
                    wt.append((s_gbf, q))
            p.op("tensor", lambda h, d=dc: h.matmul(
                     ps[0:8, 4, 0:TQ], gateT_sb[:, d, :],
                     h2f_t[:, d % 2, :],
                     start=(d == 0), stop=(d == DC - 1)),
                 waits=wt, inc=s_gmm)
            p.op("scalar", lambda h, d=dc, b=gbuf: h.copy(
                     h2g[:, b, d % 2, :], h2f_t[:, d % 2, :]), inc=s_h2v)
            if dc % 2 == 1:
                nw = p.op("sync", lambda h, qq=q, g=gi, b=gbuf: h.dma_start(
                              out=h2_sp[:, g * 2:(g + 1) * 2,
                                        qq * TQ:(qq + 1) * TQ],
                              in_=h2g[:, b, :, :]),
                          waits=[(s_h2v, s_h2v.n)], inc=s_h2w, inc_k=16)
                h2w_cnt.append(nw)
        p.op("scalar", lambda h: h.activation(
                 sc8[:], ps[0:8, 4, 0:TQ], AF.Sigmoid),
             waits=[(s_gmm, s_gmm.n), (s_trin, 4 * q)], inc=s_gsig)
        p.op("scalar", lambda h: h.nop(), inc=s_gbf)
        for tt in range(4):
            wt = [(s_gsig, s_gsig.n)] if tt == 0 else []
            if 4 * q + tt >= 2:
                wt.append((s_trv, 4 * q + tt - 1))
            p.op("tensor", lambda h, t=tt: h.transpose(
                     ps[0:128, 2 + t % 2, 0:8],
                     sc8[:, t * 128:(t + 1) * 128], eye_sb[0:8, 0:8]),
                 waits=wt, inc=s_trin)
            p.op("vector", lambda h, qq=q, t=tt: h.tensor_copy(
                     sc_sb[:, qq * 4 + t, :], ps[0:128, 2 + t % 2, 0:8]),
                 waits=[(s_trin, s_trin.n)], inc=s_trv)
    def bounce():
        p.op("scalar", lambda h: h.nop(), waits=[(s_ka, s_ka.n)], inc=s_kb)
        p.wait("vector", s_kb, s_kb.n)
    def stage(fn):
        for tt in range(NT128):
            p.op("vector", lambda h, t=tt: fn(h, t),
                 inc=(s_ka if tt == NT128 - 1 else None))
        bounce()
    for tt in range(NT128):
        p.op("vector", lambda h, t=tt: h.reduce_max(
                 mk[:, t, 0, 0:1], sc_sb[:, t, :], axis=AXX),
             waits=([(s_trv, s_trv.n)] if tt == 0 else []),
             inc=(s_ka if tt == NT128 - 1 else None))
    bounce()
    stage(lambda h, t: h.tensor_scalar(
              mk[:, t, 1, :], sc_sb[:, t, :], mk[:, t, 0, 0:1], None,
              op0=ALU.is_equal))
    stage(lambda h, t: h.tensor_tensor_scan(
              mk[:, t, 2, :], ones8[:], mk[:, t, 1, :], 0.0,
              op0=ALU.mult, op1=ALU.add))
    stage(lambda h, t: h.tensor_scalar(
              mk[:, t, 2, :], mk[:, t, 2, :], 1.0, None, op0=ALU.is_equal))
    stage(lambda h, t: h.tensor_mul(
              mk[:, t, 2, :], mk[:, t, 2, :], mk[:, t, 1, :]))
    stage(lambda h, t: h.scalar_tensor_tensor(
              mk[:, t, 3, :], mk[:, t, 2, :], -30.0, sc_sb[:, t, :],
              op0=ALU.mult, op1=ALU.add))
    stage(lambda h, t: h.reduce_max(
              mk[:, t, 0, 1:2], mk[:, t, 3, :], axis=AXX))
    stage(lambda h, t: h.tensor_scalar(
              mk[:, t, 4, :], mk[:, t, 3, :], mk[:, t, 0, 1:2], None,
              op0=ALU.is_equal))
    stage(lambda h, t: h.tensor_tensor_scan(
              mk[:, t, 5, :], ones8[:], mk[:, t, 4, :], 0.0,
              op0=ALU.mult, op1=ALU.add))
    stage(lambda h, t: h.tensor_scalar(
              mk[:, t, 5, :], mk[:, t, 5, :], 1.0, None, op0=ALU.is_equal))
    stage(lambda h, t: h.tensor_mul(
              mk[:, t, 5, :], mk[:, t, 5, :], mk[:, t, 4, :]))
    stage(lambda h, t: h.tensor_scalar_mul(
              mk[:, t, 1, :], mk[:, t, 2, :], mk[:, t, 0, 0:1]))
    for tt in range(NT128):
        p.op("vector", lambda h, t=tt: h.scalar_tensor_tensor(
                 m_sb[:, t, :], mk[:, t, 5, :], mk[:, t, 0, 1:2],
                 mk[:, t, 1, :], op0=ALU.mult, op1=ALU.add),
             inc=(s_msk if tt == NT128 - 1 else None))

    p.op("vector", lambda h: h.memset(mT_sb[:, :], 1.0))
    for tt in range(NT128):
        bk = 5 + 2 * (tt % 2)
        wt = [(s_msk, s_msk.n)]
        if tt < 2:
            wt.append((s_st2f, nq))
            wt.append((s_gbf, nq))
        else:
            wt.append((s_trpf, tt - 1))
        p.op("tensor", lambda h, t=tt, k=bk: h.transpose(
                 ps[0:8, k, 0:128], m_sb[:, t, :], eye_sb[:]),
             waits=wt, inc=s_trp)
        p.op("vector", lambda h, t=tt, k=bk: h.tensor_copy(
                 mT_sb[0:8, t * 128:(t + 1) * 128], ps[0:8, k, 0:128]),
             waits=[(s_trp, s_trp.n)], inc=s_trpf)
    nmtw = p.op("sync", lambda h: h.dma_start(out=mT_hbm[:, 0:Tn], in_=mT_sb[:]),
                waits=[(s_trpf, NT128)], inc=s_mtw, inc_k=16)

    p.wait("sync", s_x2w, x2w_cnt[-1])
    p.wait("sync", s_h2w, h2w_cnt[-1])
    ph1.close()
    p.barrier("b_ph1")

    ph2a = ES()
    sba = lambda *a: ph2a.enter_context(nc.sbuf_tensor(*a))
    h2_sb = sba("h2_sb", [128, DC, Tn], BF16)
    wslots = sba("wslots", [128, 3, DC, H], BF16)
    mrep = sba("mrep", [128, 2, Tn], BF16)
    t1sb = sba("t1sb", [128, nq, TQ], F32)
    gbuf = sba("gbuf", [128, 2, nq, TQ], BF16)

    s_h2ld = S("s_h2ld")
    s_w = S("s_w"); s_wf = S("s_wf")
    s_mrep = S("s_mrep"); s_mrepf = S("s_mrepf")
    s_u1mm = S("s_u1mm"); s_t1v = S("s_t1v"); s_silu = S("s_silu")
    s_u3mm = S("s_u3mm"); s_gv = S("s_gv"); s_gw = S("s_gw")

    nh2 = p.op("sync", lambda h: h.dma_start(out=h2_sb[:], in_=h2_sp),
               waits=[(s_h2w, h2w_cnt[-1])], inc=s_h2ld, inc_k=16)

    w_cnt = []
    mrep_cnt = []
    gw_cnt = []
    for e in range(NE):
        for (k, W) in ((2 * e, W1), (2 * e + 1, W3)):
            slot = k % 3
            wt = [(s_wf, k - 2)] if k >= 3 else []
            nwk = p.op("gpsimd", lambda h, s=slot, ee=e, WW=W: h.dma_start(
                           out=wslots[:, s, :, :],
                           in_=WW[ee].rearrange("(c p) hh -> p c hh", p=128)),
                       waits=wt, inc=s_w, inc_k=16)
            w_cnt.append(nwk)
        wt = [(s_mtw, nmtw)]
        if e >= 2:
            wt.append((s_mrepf, e - 1))
        nm = p.op("gpsimd", lambda h, ee=e: h.dma_start(
                      out=mrep[:, ee % 2, :],
                      in_=mT_hbm[ee:ee + 1, 0:Tn].partition_broadcast(128)),
                  waits=wt, inc=s_mrep, inc_k=16)
        mrep_cnt.append(nm)

    for e in range(NE):
        s1, s3 = (2 * e) % 3, (2 * e + 1) % 3
        for hc in range(HC):
            idx = e * HC + hc
            for dc in range(DC):
                wt = []
                if dc == 0:
                    wt.append((s_w, w_cnt[2 * e]))
                    if idx >= 1:
                        wt.append((s_t1v, idx))
                    if idx == 0:
                        wt.append((s_h2ld, nh2))
                for tt in range(nq):
                    p.op("tensor", lambda h, d=dc, t=tt, c=hc, s=s1: h.matmul(
                             bank(t), wslots[:, s, d, c * 128:(c + 1) * 128],
                             h2_sb[:, d, t * TQ:(t + 1) * TQ],
                             start=(d == 0), stop=(d == DC - 1)),
                         waits=(wt if tt == 0 else []),
                         inc=(s_u1mm if dc == DC - 1 and tt == nq - 1 else None))
            if hc == HC - 1:
                p.op("tensor", lambda h: h.nop(), inc=s_wf)
            for dc in range(DC):
                wt = []
                if dc == 0:
                    wt.append((s_w, w_cnt[2 * e + 1]))
                    if idx >= 1:
                        wt.append((s_gv, idx))
                for tt in range(nq):
                    p.op("tensor", lambda h, d=dc, t=tt, c=hc, s=s3: h.matmul(
                             bank(4 + t), wslots[:, s, d, c * 128:(c + 1) * 128],
                             h2_sb[:, d, t * TQ:(t + 1) * TQ],
                             start=(d == 0), stop=(d == DC - 1)),
                         waits=(wt if tt == 0 else []),
                         inc=(s_u3mm if dc == DC - 1 and tt == nq - 1 else None))
            if hc == HC - 1:
                p.op("tensor", lambda h: h.nop(), inc=s_wf)
            for tt in range(nq):
                wt = []
                if tt == 0:
                    wt = [(s_u1mm, s_u1mm.n), (s_mrep, mrep_cnt[e])]
                    if idx >= 1:
                        wt.append((s_silu, idx))
                p.op("vector", lambda h, t=tt, ee=e: h.tensor_mul(
                         t1sb[:, t, :], bank(t),
                         mrep[:, ee % 2, t * TQ:(t + 1) * TQ]),
                     waits=wt, inc=(s_t1v if tt == nq - 1 else None))
            for tt in range(nq):
                p.op("scalar", lambda h, t=tt: h.activation(
                         t1sb[:, t, :], t1sb[:, t, :], AF.Silu),
                     waits=([(s_t1v, s_t1v.n)] if tt == 0 else []),
                     inc=(s_silu if tt == nq - 1 else None))
            for tt in range(nq):
                wt = []
                if tt == 0:
                    wt = [(s_u3mm, s_u3mm.n), (s_silu, s_silu.n)]
                    if idx >= 2:
                        wt.append((s_gw, gw_cnt[idx - 2]))
                p.op("vector", lambda h, t=tt: h.tensor_mul(
                         t1sb[:, t, :], t1sb[:, t, :], bank(4 + t)), waits=wt)
                p.op("vector", lambda h, t=tt, b=hc % 2, ee=e: h.tensor_mul(
                         gbuf[:, b, t, :], t1sb[:, t, :],
                         mrep[:, ee % 2, t * TQ:(t + 1) * TQ]),
                     inc=(s_gv if tt == nq - 1 else None))
            if hc == HC - 1:
                p.op("vector", lambda h: h.nop(), inc=s_mrepf)
            ng = p.op("sync", lambda h, ee=e, c=hc, b=hc % 2: h.dma_start(
                          out=g_sp[ee, c],
                          in_=gbuf[:, b, :, :].rearrange("p a t -> p (a t)")),
                      waits=[(s_gv, s_gv.n)], inc=s_gw, inc_k=16)
            gw_cnt.append(ng)

    p.wait("sync", s_gw, gw_cnt[-1])
    ph2a.close()
    p.barrier("b_ph2a")

    ph2b = ES()
    sbb = lambda *a: ph2b.enter_context(nc.sbuf_tensor(*a))
    gpan = sbb("gpan", [128, 2, NE, HC, TQ], BF16)
    w2t = sbb("w2t", [128, 2, NE, HC, 128], BF16)
    x2b = sbb("x2b", [128, 2, TQ], F32)
    o1 = sbb("o1", [128, 2, TQ], F32)
    o2 = sbb("o2", [128, 2, TQ], F32)

    s_gpld = S("s_gpld"); s_gpf = S("s_gpf")
    s_w2 = S("s_w2"); s_w2f = S("s_w2f")
    s_omm = S("s_omm"); s_oev = S("s_oev")
    s_x2b = S("s_x2b"); s_x2bf = S("s_x2bf")
    s_ov = S("s_ov"); s_out = S("s_out")

    n_out = 0
    w2_n = {}
    x2_n = {}

    def emit_loads(gi):
        tp, dc = divmod(gi, DC)
        par = gi % 2
        wt = [(s_w2f, gi - 1)] if gi >= 2 else []
        if gi == 0:
            wt.append((s_w2c, s_w2c.n))
        w2_n[gi] = p.op("sync", lambda h, d=dc, b=par: h.dma_start(
                            out=w2t[:, b, :, :, :],
                            in_=W2b[d].rearrange("e c p j -> p e c j")),
                        waits=wt, inc=s_w2, inc_k=16)
        wt = [(s_x2bf, gi - 1)] if gi >= 2 else [(s_x2w, x2w_cnt[-1])]
        x2_n[gi] = p.op("sync", lambda h, d=dc, t=tp, b=par: h.dma_start(
                            out=x2b[:, b, :],
                            in_=x2_sp[:, d, t * TQ:(t + 1) * TQ]),
                        waits=wt, inc=s_x2b, inc_k=16)

    gp_n = {}

    def gpan_load(t, e):
        wt = []
        if t == 0:
            wt.append((s_gw, gw_cnt[(e + 1) * HC - 1]))
        elif e == 0:
            wt.append((s_gpf, t - 1))
        gp_n[(t, e)] = p.op("sync", lambda h, tt=t, ee=e: h.dma_start(
                                out=gpan[:, tt % 2, ee, :, :],
                                in_=g_sp[ee, :, :, tt * TQ:(tt + 1) * TQ]
                                .rearrange("c p t -> p c t")),
                            waits=wt, inc=s_gpld, inc_k=16)

    for tp in range(nq):
        if tp == 0:
            for e in range(NE):
                gpan_load(0, e)
        for dc in range(DC):
            gi = tp * DC + dc
            par = gi % 2
            if tp + 1 < nq and dc < NE:
                gpan_load(tp + 1, dc)
            if dc == 0:
                emit_loads(gi)
            if gi + 1 < nq * DC and (gi + 1) % DC != 0:
                emit_loads(gi + 1)
            first = True
            for e in range(NE):
                for hcc in range(HC):
                    wt = []
                    if first:
                        wt = [(s_w2, w2_n[gi])]
                        if gi >= 2:
                            wt.append((s_oev, gi - 1))
                    if hcc == 0 and gi % DC == 0:
                        wt.append((s_gpld, gp_n[(tp, e)]))
                    last = (e == NE - 1 and hcc == HC - 1)
                    p.op("tensor", lambda h, ee=e, c=hcc, b=par, g2=tp % 2,
                                 f=first:
                             h.matmul(bank(b), w2t[:, b, ee, c, :],
                                      gpan[:, g2, ee, c, :],
                                      start=f, stop=(ee == NE - 1 and c == HC - 1)),
                         waits=wt, inc=(s_omm if last else None))
                    first = False
            p.op("tensor", lambda h: h.nop(), inc=s_w2f)
            if dc == DC - 1:
                p.op("tensor", lambda h: h.nop(), inc=s_gpf)
            wt = [(s_omm, s_omm.n)]
            if gi >= 2:
                wt.append((s_ov, gi - 1))
            p.op("scalar", lambda h, d=dc, b=par: h.activation(
                     o1[:, b, :], bank(b), AF.Identity, scale=gbc[:, d:d + 1]),
                 waits=wt, inc=s_oev)
            wt = [(s_oev, s_oev.n), (s_x2b, x2_n[gi])]
            if gi >= 2:
                wt.append((s_out, 16 * (gi - 1)))
            p.op("vector", lambda h, b=par: h.tensor_add(
                     o2[:, b, :], o1[:, b, :], x2b[:, b, :]),
                 waits=wt, inc=s_ov)
            p.op("vector", lambda h: h.nop(), inc=s_x2bf)
            n_out = p.op("sync", lambda h, d=dc, t=tp, b=par: h.dma_start(
                             out=outT[d * 128:(d + 1) * 128,
                                      t * TQ:(t + 1) * TQ],
                             in_=o2[:, b, :]),
                         waits=[(s_ov, s_ov.n)], inc=s_out, inc_k=16)

    p.wait("sync", s_out, n_out)
    import kernel as _k
    _k._LAST_PROG = p
    counts = p.replay()
    nc.finalize()
    ph2b.close()
    stack.close()
    return nc, counts


def host_prep(inp, nq=4):
    Tn = nq * TQ
    f = np.float32
    shared = {}
    shared["wmod"] = np.ascontiguousarray(np.asarray(inp["L0_wmod"], f))
    shared["gateT"] = np.ascontiguousarray(np.asarray(inp["L0_ffn_gate_w"], f).T)
    shared["xprojT"] = np.ascontiguousarray(
        np.asarray(inp["L0_attn_xproj_w"], f).T)
    shared["dtprojT"] = np.ascontiguousarray(
        np.asarray(inp["L0_attn_dtproj_w"], f).T)
    shared["dtb"] = np.ascontiguousarray(
        np.asarray(inp["L0_attn_dtproj_b"], f).reshape(DC, 128).T)
    shared["convw"] = np.ascontiguousarray(
        np.asarray(inp["L0_attn_conv_w"], f).reshape(D, 4)
        .reshape(DC, 128, 4).transpose(1, 0, 2))
    shared["convb"] = np.ascontiguousarray(
        np.asarray(inp["L0_attn_conv_b"], f).reshape(DC, 128).T)
    shared["alog"] = np.ascontiguousarray(
        np.asarray(inp["L0_attn_A_log"], f).reshape(DC, 128, NST)
        .transpose(1, 0, 2))
    shared["dskip"] = np.ascontiguousarray(
        np.asarray(inp["L0_attn_D_skip"], f).reshape(DC, 128).T)
    w1 = np.asarray(inp["L0_ffn_w1"], f)
    w3 = np.asarray(inp["L0_ffn_w3"], f)
    w2 = np.asarray(inp["L0_ffn_w2"], f)
    ws1 = np.asarray(inp["L0_ffn_ws1"], f)
    ws3 = np.asarray(inp["L0_ffn_ws3"], f)
    ws2 = np.asarray(inp["L0_ffn_ws2"], f)
    W1 = np.concatenate([w1, ws1[None]], 0).transpose(0, 2, 1)
    W3 = np.concatenate([w3, ws3[None]], 0).transpose(0, 2, 1)
    W2T = np.concatenate([w2, ws2[None]], 0).transpose(0, 2, 1)
    shared["W1"] = np.ascontiguousarray(W1)
    shared["W3"] = np.ascontiguousarray(W3)
    shared["W2s"] = np.ascontiguousarray(
        W2T.reshape(NE, HC, 128, DC, 128).transpose(3, 0, 1, 2, 4))
    shared["eye"] = np.eye(128, dtype=f)
    shared["onesv"] = np.ones((128, 1), dtype=f)

    x = np.asarray(inp["L0_attn_x"], f)
    c = np.asarray(inp["L0_c"], f)
    per_core = []
    for i in range(8):
        m = {}
        m["xT"] = np.ascontiguousarray(x[i, :Tn, :].T)
        m["c16"] = np.ascontiguousarray(c[i].reshape(DC, 128).T)
        per_core.append(m)
    return shared, per_core


def kernel(**inp):
    nq = 4
    key = ("prog", nq)
    if key not in _CACHE:
        _CACHE[key] = build(nq)
    nc, _ = _CACHE[key]
    shared, per_core = host_prep(inp, nq)
    in_maps = [dict(shared, **pc) for pc in per_core]
    res = run_bass_kernel_spmd(nc, in_maps, core_ids=list(range(8)))
    out = np.empty((8, nq * TQ, D), dtype=np.float32)
    for i in range(8):
        out[i] = np.asarray(res.results[i]["outT"]).T
    return out
